# version 02
# baseline (speedup 1.0000x reference)
import numpy as np
import concourse.bass as bass
import concourse.mybir as mybir

from collections import defaultdict

ENGINES = ("tensor", "vector", "scalar", "gpsimd", "sync")


class BK:
    def __init__(self, nc, same_engine_sems=True):
        self.nc = nc
        self.same = same_engine_sems
        self.ops = []
        self.last_write = {}
        self.readers = defaultdict(list)
        self.chan_count = defaultdict(int)
        self._barrier_ops = []
        self._since_barrier = []

    def op(self, engine, fn, reads=(), writes=(), dma=False, channel=None, tag=""):
        assert engine in ENGINES, engine
        i = len(self.ops)
        deps = set(self._barrier_ops)
        for r in reads:
            w = self.last_write.get(r)
            if w is not None:
                deps.add(w)
        for r in writes:
            w = self.last_write.get(r)
            if w is not None:
                deps.add(w)
            deps.update(self.readers[r])
        deps.discard(i)
        self._since_barrier.append(i)
        if dma:
            channel = channel if channel is not None else engine
            self.chan_count[channel] += 1
        rec = dict(i=i, engine=engine, fn=fn, deps=sorted(deps), dma=dma,
                   channel=channel, needs_inc=dma, inc_val=None, tag=tag,
                   chan_snap=None)
        self.ops.append(rec)
        for r in writes:
            self.last_write[r] = i
            self.readers[r] = []
        for r in reads:
            self.readers[r].append(i)
        return i

    def dma(self, engine, fn, reads=(), writes=(), channel=None, tag=""):
        return self.op(engine, fn, reads=reads, writes=writes, dma=True,
                       channel=channel, tag=tag)

    def cc(self, fn, reads=(), writes=(), tag=""):
        i = self.op("gpsimd", fn, reads=reads, writes=writes, dma=True,
                    channel="cc", tag=tag)
        self.ops[i]["is_cc"] = True
        return i

    def barrier(self, engines=ENGINES):
        last_eng = {}
        last_dma = {}
        for i in self._since_barrier:
            o = self.ops[i]
            if o["dma"]:
                last_dma[o["channel"]] = i
            else:
                last_eng[o["engine"]] = i
        prior = sorted(set(last_eng.values()) | set(last_dma.values()))
        self._since_barrier = []
        bids = []
        for e in engines:
            i = self.op(e, lambda eng: eng.nop(), tag=f"barrier:{e}")
            self.ops[i]["deps"] = sorted(set(self.ops[i]["deps"]) | set(prior))
            self._since_barrier.pop()
            bids.append(i)
        self._barrier_ops = bids

    def _edge_needed(self, prod, cons):
        if prod["dma"] or cons["dma"]:
            return True
        if prod["engine"] == "tensor" and cons["engine"] == "tensor":
            return False
        if prod["engine"] == cons["engine"] and not self.same:
            return False
        return True

    def finish(self, final_engine="sync"):
        nc = self.nc
        ops = self.ops
        cc = defaultdict(int)
        snaps_cc = []
        for o in ops:
            snaps_cc.append(dict(cc))
            if o["dma"]:
                cc[o["channel"]] += 1 if o.get("is_cc") else 16
        for o, s in zip(ops, snaps_cc):
            o["chan_snap"] = s
        for o in ops:
            for d in o["deps"]:
                if self._edge_needed(ops[d], o) and not ops[d]["dma"]:
                    ops[d]["needs_inc"] = True
        cnt = defaultdict(int)
        for o in ops:
            if o["needs_inc"]:
                if o["dma"]:
                    cnt["d:" + o["channel"]] += 1 if o.get("is_cc") else 16
                    o["inc_val"] = cnt["d:" + o["channel"]]
                else:
                    cnt["c:" + o["engine"]] += 1
                    o["inc_val"] = cnt["c:" + o["engine"]]
        know = {e: {} for e in ENGINES}
        snaps = {}
        for o in ops:
            e = o["engine"]
            waits = {}
            if o["dma"]:
                prev = o["chan_snap"].get(o["channel"], 0)
                sem0 = "d:" + o["channel"]
                if prev > 0 and know[e].get(sem0, 0) < prev:
                    waits[sem0] = prev
            for d in o["deps"]:
                p = ops[d]
                if self._edge_needed(p, o):
                    if p["dma"]:
                        sem = "d:" + p["channel"]
                        v = o["chan_snap"].get(p["channel"], 0)
                        assert v >= p["inc_val"]
                    else:
                        sem = "c:" + p["engine"]
                        v = p["inc_val"]
                        assert v is not None, (p["tag"], o["tag"])
                    if know[e].get(sem, 0) < v:
                        waits[sem] = max(waits.get(sem, 0), v)
                for s2, v2 in snaps.get(d, {}).items():
                    if know[e].get(s2, 0) < v2:
                        know[e][s2] = v2
            o["waits"] = sorted(waits.items())
            for sem, v in waits.items():
                if know[e].get(sem, 0) < v:
                    know[e][sem] = v
            if o["needs_inc"]:
                sem = ("d:" + o["channel"]) if o["dma"] else ("c:" + e)
                k = dict(know[e])
                k[sem] = o["inc_val"]
                snaps[o["i"]] = k
        finals = sorted(cnt.items())
        from contextlib import ExitStack
        ctx = ExitStack()
        sems = {}
        for name in sorted(cnt):
            sems[name] = ctx.enter_context(nc.semaphore(name.replace(":", "_")))
        block = ctx.enter_context(nc.Block())
        per_engine = {e: [o for o in ops if o["engine"] == e] for e in ENGINES}

        def mk(engine):
            def body(eng):
                for o in per_engine[engine]:
                    for sem, v in o["waits"]:
                        eng.wait_ge(sems[sem], v)
                    instr = o["fn"](eng)
                    if o["needs_inc"]:
                        sem = ("d:" + o["channel"]) if o["dma"] else ("c:" + engine)
                        assert instr is not None, f"op {o['tag']} returned None but needs inc"
                        instr.then_inc(sems[sem], (1 if o.get("is_cc") else 16) if o["dma"] else 1)
                if engine == final_engine:
                    for k, v in finals:
                        eng.wait_ge(sems[k], v)
            return body

        for e in ENGINES:
            if per_engine[e] or e == final_engine:
                getattr(block, e)(mk(e))
        ctx.close()
        return dict(cnt)


F32 = mybir.dt.float32
BF16 = mybir.dt.bfloat16
U32 = mybir.dt.uint32
AF = mybir.ActivationFunctionType
ALU = mybir.AluOpType
IOA = bass.IndirectOffsetOnAxis

D = 2048
T = 2048
DT = 16
NJ = 4
TQ = 512
H = 8
LAMBDA_INIT = 0.8 - 0.6 * float(np.exp(-0.3 * 12))


def host_constants_a():
    masks = np.zeros((4, 128, TQ), dtype=np.float32)
    kt = np.arange(128)[:, None]
    qt = np.arange(TQ)[None, :]
    for m in range(4):
        masks[m] = (kt <= qt - 128 * m).astype(np.float32)
    return masks


class Common:
    def __init__(self, nc, ctx, bk):
        E = ctx.enter_context
        self.pb = [E(nc.psum_tensor(f"pb{i}", [128, 512], F32)) for i in range(8)]
        self.ones_col_bf = E(nc.sbuf_tensor("ones_col_bf", [128, 1], BF16))
        self.ones_col_f = E(nc.sbuf_tensor("ones_col_f", [128, 1], F32))
        self.ones_row_f = E(nc.sbuf_tensor("ones_row_f", [1, 128], F32))
        bk.op("vector", lambda v: v.memset(self.ones_col_bf[:], 1.0), writes=["ones_col_bf"])
        bk.op("vector", lambda v: v.memset(self.ones_col_f[:], 1.0), writes=["ones_col_f"])
        bk.op("vector", lambda v: v.memset(self.ones_row_f[:], 1.0), writes=["ones_row_f"])

    def bcast_row(self, bk, bank, row_ap, row_res, n, tag="bc"):
        pb = self.pb[bank]
        rr = [row_res] if isinstance(row_res, str) else list(row_res)
        bk.op("tensor", lambda t: t.matmul(pb[:, 0:n], self.ones_row_f[:], row_ap,
                                           start=True, stop=True),
              reads=["ones_row_f"] + rr, writes=[f"pb{bank}"], tag=tag)


def emit_l0_attn(nc, bk, ctx, com, io):
    E = ctx.enter_context
    pb = com.pb
    xT, Wq, Wk, Wv, Wo = io["xT"], io["Wq"], io["Wk"], io["Wv"], io["Wo"]
    abuf, x1T = io["abuf"], io["x1T"]
    SCALE = 1.0 / np.sqrt(128.0)

    lam_col = E(nc.sbuf_tensor("lam_col", [1, 1], F32))
    subw = E(nc.sbuf_tensor("subw", [128, 2], F32))
    msk = E(nc.sbuf_tensor("msk", [128, 4, 512], BF16))

    with nc.sbuf_tensor("lamv", [128, 4], F32) as lamv, \
         nc.sbuf_tensor("lamprod", [128, 2], F32) as lamprod, \
         nc.sbuf_tensor("lamrow", [1, 4], F32) as lamrow:
        for i, nm in enumerate(["lq1", "lk1", "lq2", "lk2"]):
            bk.dma("gpsimd", lambda g, i=i, nm=nm: g.dma_start(out=lamv[:, i:i + 1], in_=io[nm]),
                   reads=["i"], writes=["lamv"], channel="c")
        bk.dma("gpsimd", lambda g: g.dma_start(out=subw[:, 0:1], in_=io["subln"][0:128, :]),
               reads=["i"], writes=["subw"], channel="c")
        bk.dma("gpsimd", lambda g: g.dma_start(out=subw[:, 1:2], in_=io["subln"][128:256, :]),
               reads=["i"], writes=["subw"], channel="c")
        bk.dma("gpsimd", lambda g: g.dma_start(out=msk[:], in_=io["masks"].rearrange("m p q -> p m q")),
               reads=["i"], writes=["msk"], channel="c")
        bk.op("vector", lambda v: v.tensor_tensor(out=lamprod[:, 0:1], in0=lamv[:, 0:1], in1=lamv[:, 1:2], op=ALU.mult),
              reads=["lamv"], writes=["lamprod"])
        bk.op("vector", lambda v: v.tensor_tensor(out=lamprod[:, 1:2], in0=lamv[:, 2:3], in1=lamv[:, 3:4], op=ALU.mult),
              reads=["lamv"], writes=["lamprod"])
        bk.op("tensor", lambda t: t.matmul(pb[5][0:1, 0:2], com.ones_col_f[:], lamprod[:], start=True, stop=True),
              reads=["ones_col_f", "lamprod"], writes=["pb5"])
        bk.op("scalar", lambda s: s.activation(out=lamrow[0:1, 0:2], in_=pb[5][0:1, 0:2], func=AF.Exp),
              reads=["pb5"], writes=["lamrow"])
        bk.op("vector", lambda v: v.tensor_tensor(out=lamrow[0:1, 2:3], in0=lamrow[0:1, 1:2], in1=lamrow[0:1, 0:1],
                                                  op=ALU.subtract),
              reads=["lamrow"], writes=["lamrow"])
        bk.op("vector", lambda v: v.tensor_scalar(out=lam_col[:], in0=lamrow[0:1, 2:3], scalar1=-LAMBDA_INIT,
                                                  scalar2=None, op0=ALU.add),
              reads=["lamrow"], writes=["lam_col"])
        bk.op("vector", lambda v: v.tensor_scalar(out=subw[:], in0=subw[:], scalar1=1.0 - LAMBDA_INIT,
                                                  scalar2=None, op0=ALU.mult),
              reads=["subw"], writes=["subw"])

    bk.barrier()

    with nc.sbuf_tensor("xTb", [128, DT, T], BF16) as xTb, \
         nc.sbuf_tensor("wq_sb", [128, 2, DT, 256], BF16) as wq_sb, \
         nc.sbuf_tensor("wk_sb", [128, 2, DT, 256], BF16) as wk_sb, \
         nc.sbuf_tensor("wv_sb", [128, 2, DT, 256], BF16) as wv_sb, \
         nc.sbuf_tensor("qk_sb", [128, 2, 2, T], BF16) as qk_sb, \
         nc.sbuf_tensor("v_sb", [128, DT, 256], BF16) as v_sb, \
         nc.sbuf_tensor("exp_sb", [128, 2, 512], BF16) as exp_sb, \
         nc.sbuf_tensor("o_sb", [128, 2, 2, 512], F32) as o_sb, \
         nc.sbuf_tensor("drow", [1, 2, 512], F32) as drow, \
         nc.sbuf_tensor("rrow", [1, 4, 512], F32) as rrow, \
         nc.sbuf_tensor("a_sb", [128, 2, 512], F32) as a_sb, \
         nc.sbuf_tensor("an_sb", [128, 2, 512], BF16) as an_sb, \
         nc.sbuf_tensor("sq_sb", [128, 2, 512], BF16) as sq_sb:

        for a in range(DT):
            bk.dma("gpsimd", lambda g, a=a: g.dma_start(out=xTb[:, a, :], in_=xT[a * 128:(a + 1) * 128, :]),
                   reads=["i"], writes=[f"xTb{a}"], channel="x", tag=f"ldxT{a}")

        for h in range(H):
            sl = h % 2
            for (wsb, W, res) in ((wq_sb, Wq, "wq"), (wk_sb, Wk, "wk"), (wv_sb, Wv, "wv")):
                bk.dma("gpsimd", lambda g, wsb=wsb, W=W, sl=sl, h=h: g.dma_start(
                    out=wsb[:, sl, :, :],
                    in_=W.rearrange("(a p) f -> p a f", p=128)[:, :, h * 256:(h + 1) * 256]),
                    reads=["i"], writes=[f"{res}{sl}"], channel=f"w{sl}", tag=f"ldw{res}{h}")
            for which, wres, wsb in ((0, "wq", wq_sb), (1, "wk", wk_sb)):
                for ft in range(2):
                    for ng in range(4):
                        bank = 6 + (ng % 2)
                        for a in range(DT):
                            bk.op("tensor", lambda t, a=a, ft=ft, ng=ng, sl=sl, wsb=wsb, bank=bank: t.matmul(
                                pb[bank][:, 0:512],
                                wsb[:, sl, a, ft * 128:(ft + 1) * 128],
                                xTb[:, a, ng * 512:(ng + 1) * 512],
                                start=(a == 0), stop=(a == DT - 1)),
                                reads=[f"{wres}{sl}", f"xTb{a}"], writes=[f"pb{bank}"], tag="proj")
                        bk.op("scalar", lambda s, which=which, ft=ft, ng=ng, bank=bank: s.copy(
                            out=qk_sb[:, which, ft, ng * 512:(ng + 1) * 512], in_=pb[bank][:, 0:512]),
                            reads=[f"pb{bank}"], writes=[f"qk{which}{ft}"], tag="projevac")
            for tt in range(DT):
                bank = 6 + (tt % 2)
                for a in range(DT):
                    bk.op("tensor", lambda t, a=a, tt=tt, sl=sl, bank=bank: t.matmul(
                        pb[bank][:, 0:256],
                        xTb[:, a, tt * 128:(tt + 1) * 128],
                        wv_sb[:, sl, a, :],
                        start=(a == 0), stop=(a == DT - 1)),
                        reads=[f"wv{sl}", f"xTb{a}"], writes=[f"pb{bank}"], tag="projv")
                bk.op("scalar", lambda s, tt=tt, bank=bank: s.copy(
                    out=v_sb[:, tt, :], in_=pb[bank][:, 0:256]),
                    reads=[f"pb{bank}"], writes=[f"v{tt}"], tag="projvevac")

            for J in range(NJ):
                nI = 4 * J + 4
                for beta in range(2):
                    for I in range(nI):
                        sb = I % 2
                        bk.op("tensor", lambda t, I=I, J=J, beta=beta, sb=sb: t.matmul(
                            pb[sb][:, 0:512],
                            qk_sb[:, 1, beta, I * 128:(I + 1) * 128],
                            qk_sb[:, 0, beta, J * 512:(J + 1) * 512],
                            start=True, stop=True),
                            reads=[f"qk1{beta}", f"qk0{beta}"], writes=[f"pb{sb}"], tag="qk")
                        bk.op("scalar", lambda s, sb=sb: s.activation(
                            out=exp_sb[:, sb, :], in_=pb[sb][:, 0:512], func=AF.Exp, scale=SCALE),
                            reads=[f"pb{sb}"], writes=[f"exp{sb}"], tag="exp")
                        m = I - 4 * J
                        if m >= 0:
                            bk.op("vector", lambda v, sb=sb, m=m: v.tensor_tensor(
                                out=exp_sb[:, sb, :], in0=exp_sb[:, sb, :], in1=msk[:, m, :], op=ALU.mult),
                                reads=[f"exp{sb}", "msk"], writes=[f"exp{sb}"], tag="mask")
                        bk.op("tensor", lambda t, sb=sb, I=I, nI=nI: t.matmul(
                            pb[4][0:1, 0:512], com.ones_col_bf[:], exp_sb[:, sb, :],
                            start=(I == 0), stop=(I == nI - 1)),
                            reads=["ones_col_bf", f"exp{sb}"], writes=["pb4"], tag="den")
                        for eta in range(2):
                            bk.op("tensor", lambda t, sb=sb, I=I, nI=nI, eta=eta: t.matmul(
                                pb[2 + eta][:, 0:512],
                                v_sb[:, I, eta * 128:(eta + 1) * 128],
                                exp_sb[:, sb, :],
                                start=(I == 0), stop=(I == nI - 1)),
                                reads=[f"v{I}", f"exp{sb}"], writes=[f"pb{2 + eta}"], tag="pv")
                    bk.op("scalar", lambda s, beta=beta: s.copy(out=drow[:, beta, :], in_=pb[4][0:1, 0:512]),
                          reads=["pb4"], writes=[f"drow{beta}"], tag="dcopy")
                    for eta in range(2):
                        bk.op("scalar", lambda s, beta=beta, eta=eta: s.copy(
                            out=o_sb[:, beta, eta, :], in_=pb[2 + eta][:, 0:512]),
                            reads=[f"pb{2 + eta}"], writes=[f"osb{beta}{eta}"], tag="pvevac")
                bk.op("scalar", lambda s: s.activation(out=rrow[0:1, 0, :], in_=drow[0:1, 1, :], func=AF.Ln),
                      reads=["drow1"], writes=["rr0"], tag="lnd")
                bk.op("scalar", lambda s: s.activation(out=rrow[0:1, 0, :], in_=rrow[0:1, 0, :], func=AF.Exp, scale=-1.0),
                      reads=["rr0"], writes=["rr0"], tag="expd")
                bk.op("vector", lambda v: v.tensor_tensor(out=rrow[0:1, 1, :], in0=drow[0:1, 0, :],
                                                          in1=rrow[0:1, 0, :], op=ALU.mult),
                      reads=["drow0", "rr0"], writes=["rr1"], tag="ratio")
                bk.op("vector", lambda v: v.tensor_scalar(out=rrow[0:1, 1, :], in0=rrow[0:1, 1, :],
                                                          scalar1=lam_col[:], scalar2=None, op0=ALU.mult),
                      reads=["rr1", "lam_col"], writes=["rr1"], tag="ratlam")
                com.bcast_row(bk, 5, rrow[0:1, 1, :], "rr1", 512, tag="rlb")
                for eta in range(2):
                    bk.op("vector", lambda v, eta=eta: v.tensor_tensor(
                        out=a_sb[:, eta, :], in0=o_sb[:, 1, eta, :], in1=pb[5][:, 0:512], op=ALU.mult),
                        reads=[f"osb1{eta}", "pb5"], writes=["asb"], tag="a1")
                    bk.op("vector", lambda v, eta=eta: v.tensor_tensor(
                        out=a_sb[:, eta, :], in0=a_sb[:, eta, :], in1=o_sb[:, 0, eta, :], op=ALU.add),
                        reads=["asb", f"osb0{eta}"], writes=["asb"], tag="a2")
                for eta in range(2):
                    bk.op("vector", lambda v, eta=eta: v.tensor_tensor(
                        out=sq_sb[:, eta, :], in0=a_sb[:, eta, :], in1=a_sb[:, eta, :], op=ALU.mult),
                        reads=["asb"], writes=["sqsb"], tag="sq")
                for eta in range(2):
                    bk.op("tensor", lambda t, eta=eta: t.matmul(
                        pb[4][0:1, 0:512], com.ones_col_bf[:], sq_sb[:, eta, :],
                        start=(eta == 0), stop=(eta == 1)),
                        reads=["ones_col_bf", "sqsb"], writes=["pb4"], tag="ssum")
                bk.op("vector", lambda v: v.tensor_tensor(out=rrow[0:1, 2, :], in0=drow[0:1, 0, :],
                                                          in1=drow[0:1, 0, :], op=ALU.mult),
                      reads=["drow0"], writes=["rr2"], tag="d0sq")
                bk.op("vector", lambda v: v.scalar_tensor_tensor(
                    out=rrow[0:1, 2, :], in0=rrow[0:1, 2, :], scalar=256.0 * 1e-5,
                    in1=pb[4][0:1, 0:512], op0=ALU.mult, op1=ALU.add),
                    reads=["rr2", "pb4"], writes=["rr2"], tag="epsadd")
                bk.op("scalar", lambda s: s.activation(out=rrow[0:1, 2, :], in_=rrow[0:1, 2, :],
                                                       func=AF.Ln, scale=1.0 / 256.0),
                      reads=["rr2"], writes=["rr2"], tag="lnu")
                bk.op("scalar", lambda s: s.activation(out=rrow[0:1, 2, :], in_=rrow[0:1, 2, :],
                                                       func=AF.Exp, scale=-0.5),
                      reads=["rr2"], writes=["rr2"], tag="rstd")
                com.bcast_row(bk, 5, rrow[0:1, 2, :], "rr2", 512, tag="rstdb")
                for eta in range(2):
                    bk.op("vector", lambda v, eta=eta: v.tensor_tensor(
                        out=a_sb[:, eta, :], in0=a_sb[:, eta, :], in1=pb[5][:, 0:512], op=ALU.mult),
                        reads=["asb", "pb5"], writes=["asb"], tag="rmul")
                    bk.op("vector", lambda v, eta=eta: v.tensor_scalar(
                        out=an_sb[:, eta, :], in0=a_sb[:, eta, :], scalar1=subw[:, eta:eta + 1],
                        scalar2=None, op0=ALU.mult),
                        reads=["asb", "subw"], writes=["ansb"], tag="subwm")
                bk.dma("sync", lambda s, h=h, J=J: s.dma_start(
                    out=abuf.rearrange("(e p) t -> p e t", p=128)[:, 2 * h:2 * h + 2, J * 512:(J + 1) * 512],
                    in_=an_sb[:]),
                    reads=["ansb"], writes=[f"abuf{h}{J}"], channel="ab", tag="stab")

    bk.barrier()

    with nc.sbuf_tensor("ab_full", [128, DT, T], BF16) as ab_full, \
         nc.sbuf_tensor("wo_full", [128, DT, D], BF16) as wo_full, \
         nc.sbuf_tensor("x1pre", [128, DT, 512], F32) as x1pre, \
         nc.sbuf_tensor("xf_sb", [128, 2, 512], F32) as xf_sb, \
         nc.sbuf_tensor("pre_bf", [128, 2, 512], BF16) as pre_bf, \
         nc.sbuf_tensor("sq2_bf", [128, 2, 512], BF16) as sq2_bf, \
         nc.sbuf_tensor("st_row", [1, 4, 512], F32) as st_row, \
         nc.sbuf_tensor("x1f", [128, 2, 512], F32) as x1f:
        bk.dma("gpsimd", lambda g: g.dma_start(
            out=ab_full[:], in_=abuf.rearrange("(e p) t -> p e t", p=128)),
            reads=[f"abuf{h}{J}" for h in range(H) for J in range(NJ)],
            writes=["abf"], channel="ab2", tag="ldab")
        bk.dma("gpsimd", lambda g: g.dma_start(
            out=wo_full[:], in_=Wo.rearrange("(a p) f -> p a f", p=128)),
            reads=["i"], writes=["wof"], channel="ab2", tag="ldwo")
        emit_postln(nc, bk, com, xT, x1T, DT,
                    mk_matmul=lambda bank, Dt, J: [
                        bk.op("tensor", lambda t, Dt=Dt, J=J, et=et, bank=bank: t.matmul(
                            pb[bank][:, 0:512],
                            wo_full[:, et, Dt * 128:(Dt + 1) * 128],
                            ab_full[:, et, J * 512:(J + 1) * 512],
                            start=(et == 0), stop=(et == DT - 1)),
                            reads=["wof", "abf"], writes=[f"pb{bank}"], tag="wo")
                        for et in range(DT)],
                    bufs=(x1pre, xf_sb, pre_bf, sq2_bf, st_row, x1f))


def emit_postln(nc, bk, com, res_in, res_out, n_dt, mk_matmul, bufs, x_out_bf=None,
                eps=1e-5, prep_j=None, scale_cols=None, scale_res=None):
    pb = com.pb
    x1pre, xf_sb, pre_bf, sq2_bf, st_row, x1f = bufs
    for J in range(NJ):
        if prep_j is not None:
            prep_j(J)
        for Dt in range(n_dt):
            bank = 6 + (Dt % 2)
            mk_matmul(bank, Dt, J)
            bk.dma("sync", lambda s, Dt=Dt, J=J: s.dma_start(
                out=xf_sb[:, Dt % 2, :], in_=res_in[Dt * 128:(Dt + 1) * 128, J * 512:(J + 1) * 512]),
                reads=["i"], writes=[f"xf{Dt % 2}"], channel=f"xf{Dt % 2}", tag="ldxf")
            if scale_cols is None:
                bk.op("vector", lambda v, Dt=Dt, bank=bank: v.tensor_tensor(
                    out=x1pre[:, Dt, :], in0=pb[bank][:, 0:512], in1=xf_sb[:, Dt % 2, :], op=ALU.add),
                    reads=[f"pb{bank}", f"xf{Dt % 2}"], writes=[f"x1pre{Dt}"], tag="resadd")
            else:
                bk.op("vector", lambda v, Dt=Dt, bank=bank: v.scalar_tensor_tensor(
                    out=x1pre[:, Dt, :], in0=pb[bank][:, 0:512], scalar=scale_cols(Dt),
                    in1=xf_sb[:, Dt % 2, :], op0=ALU.mult, op1=ALU.add),
                    reads=[f"pb{bank}", f"xf{Dt % 2}", scale_res], writes=[f"x1pre{Dt}"], tag="resadds")
            bk.op("vector", lambda v, Dt=Dt: v.tensor_copy(out=pre_bf[:, Dt % 2, :], in_=x1pre[:, Dt, :]),
                  reads=[f"x1pre{Dt}"], writes=[f"prebf{Dt % 2}"], tag="precast")
            bk.op("vector", lambda v, Dt=Dt: v.tensor_tensor(
                out=sq2_bf[:, Dt % 2, :], in0=x1pre[:, Dt, :], in1=x1pre[:, Dt, :], op=ALU.mult),
                reads=[f"x1pre{Dt}"], writes=[f"sq2{Dt % 2}"], tag="sq2")
            bk.op("tensor", lambda t, Dt=Dt: t.matmul(
                pb[4][0:1, 0:512], com.ones_col_bf[:], pre_bf[:, Dt % 2, :],
                start=(Dt == 0), stop=(Dt == n_dt - 1)),
                reads=["ones_col_bf", f"prebf{Dt % 2}"], writes=["pb4"], tag="lnsum")
            bk.op("tensor", lambda t, Dt=Dt: t.matmul(
                pb[3][0:1, 0:512], com.ones_col_bf[:], sq2_bf[:, Dt % 2, :],
                start=(Dt == 0), stop=(Dt == n_dt - 1)),
                reads=["ones_col_bf", f"sq2{Dt % 2}"], writes=["pb3"], tag="lnsq")
        N = n_dt * 128.0
        bk.op("scalar", lambda s: s.mul(st_row[0:1, 0, :], pb[4][0:1, 0:512], 1.0 / N),
              reads=["pb4"], writes=["st0"], tag="m")
        bk.op("scalar", lambda s: s.activation(out=st_row[0:1, 1, :], in_=st_row[0:1, 0, :], func=AF.Square),
              reads=["st0"], writes=["st1"], tag="msq")
        bk.op("vector", lambda v: v.scalar_tensor_tensor(
            out=st_row[0:1, 1, :], in0=pb[3][0:1, 0:512], scalar=1.0 / N, in1=st_row[0:1, 1, :],
            op0=ALU.mult, op1=ALU.subtract),
            reads=["pb3", "st1"], writes=["st1"], tag="var")
        bk.op("vector", lambda v: v.tensor_scalar(out=st_row[0:1, 1, :], in0=st_row[0:1, 1, :],
                                                  scalar1=eps, scalar2=None, op0=ALU.add),
              reads=["st1"], writes=["st1"], tag="epsadd2")
        bk.op("scalar", lambda s: s.activation(out=st_row[0:1, 1, :], in_=st_row[0:1, 1, :], func=AF.Ln),
              reads=["st1"], writes=["st1"], tag="lnvar")
        bk.op("scalar", lambda s: s.activation(out=st_row[0:1, 1, :], in_=st_row[0:1, 1, :],
                                               func=AF.Exp, scale=-0.5),
              reads=["st1"], writes=["st1"], tag="rstd2")
        bk.op("vector", lambda v: v.scalar_tensor_tensor(
            out=st_row[0:1, 2, :], in0=st_row[0:1, 0, :], scalar=-1.0, in1=st_row[0:1, 1, :],
            op0=ALU.mult, op1=ALU.mult),
            reads=["st0", "st1"], writes=["st2"], tag="shift")
        com.bcast_row(bk, 5, st_row[0:1, 1, :], "st1", 512, tag="rstdb2")
        com.bcast_row(bk, 2, st_row[0:1, 2, :], "st2", 512, tag="shiftb")
        for Dt in range(n_dt):
            bk.op("vector", lambda v, Dt=Dt: v.tensor_tensor(
                out=x1f[:, Dt % 2, :], in0=x1pre[:, Dt, :], in1=pb[5][:, 0:512], op=ALU.mult),
                reads=[f"x1pre{Dt}", "pb5"], writes=[f"x1f{Dt % 2}"], tag="n1")
            bk.op("vector", lambda v, Dt=Dt: v.tensor_tensor(
                out=x1f[:, Dt % 2, :], in0=x1f[:, Dt % 2, :], in1=pb[2][:, 0:512], op=ALU.add),
                reads=[f"x1f{Dt % 2}", "pb2"], writes=[f"x1f{Dt % 2}"], tag="n2")
            bk.dma("sync", lambda s, Dt=Dt, J=J: s.dma_start(
                out=res_out[Dt * 128:(Dt + 1) * 128, J * 512:(J + 1) * 512], in_=x1f[:, Dt % 2, :]),
                reads=[f"x1f{Dt % 2}"], writes=[f"o:x1T{Dt}{J}"], channel=f"xo{Dt % 2}", tag="stx1")
            if x_out_bf is not None:
                bk.op("vector", lambda v, Dt=Dt, J=J: v.tensor_copy(
                    out=x_out_bf[:, Dt, J * 512:(J + 1) * 512], in_=x1f[:, Dt % 2, :]),
                    reads=[f"x1f{Dt % 2}"], writes=["RESBF"], tag="x1bf")


def host_constants_b():
    H = np.array([[1.0]], dtype=np.float32)
    while H.shape[0] < 128:
        H = np.block([[H, H], [H, -H]])
    return H.astype(np.float32)


def emit_butterfly(nc, bk, w_ap, wres, n_tiles, fcols, tag):
    lev = 1
    while lev < n_tiles:
        npair = n_tiles // (2 * lev)
        ev = w_ap.rearrange("p (hi two lo) f -> p hi two lo f", two=2, lo=lev)
        bk.op("vector", lambda v, ev=ev: v.tensor_tensor(
            out=ev[:, :, 1, :, :], in0=ev[:, :, 0, :, :], in1=ev[:, :, 1, :, :], op=ALU.subtract),
            reads=[wres], writes=[wres], tag=f"{tag}bs{lev}")
        bk.op("vector", lambda v, ev=ev: v.scalar_tensor_tensor(
            out=ev[:, :, 0, :, :], in0=ev[:, :, 0, :, :], scalar=2.0,
            in1=ev[:, :, 1, :, :], op0=ALU.mult, op1=ALU.subtract),
            reads=[wres], writes=[wres], tag=f"{tag}ba{lev}")
        lev *= 2


def emit_l0_ffn(nc, bk, ctx, com, io):
    E = ctx.enter_context
    pb = com.pb
    x1T, x2T, mT = io["x1T"], io["x2T"], io["mT"]
    IS = 1.0 / np.sqrt(2048.0)
    IS2 = 1.0 / np.sqrt(8192.0)

    H128_bf = E(nc.sbuf_tensor("H128_bf", [128, 128], BF16))
    wsd_eff = E(nc.sbuf_tensor("wsd_eff", [128, DT], F32))
    bk.dma("gpsimd", lambda g: g.dma_start(out=H128_bf[:], in_=io["H128"]),
           reads=["i"], writes=["H128"], channel="c")
    bk.dma("gpsimd", lambda g: g.dma_start(out=wsd_eff[:], in_=io["Ws_dp"]),
        reads=["i"], writes=["wsd"], channel="c")
    bk.op("vector", lambda v: v.tensor_scalar(out=wsd_eff[:], in0=wsd_eff[:], scalar1=IS2,
                                              scalar2=None, op0=ALU.mult),
          reads=["wsd"], writes=["wsd"], tag="wsdeff")

    from contextlib import ExitStack
    zctx = ExitStack()
    z_sb = zctx.enter_context(nc.sbuf_tensor("z_sb", [128, DT, T], BF16))
    with nc.sbuf_tensor("su_ug_sb", [128, DT], F32) as su_ug_sb, \
         nc.sbuf_tensor("h_su", [128, DT, 128], BF16) as h_su, \
         nc.sbuf_tensor("x1b", [128, 2, 512], BF16) as x1b:
        bk.dma("gpsimd", lambda g: g.dma_start(out=su_ug_sb[:], in_=io["SU_ug"]),
            reads=["i"], writes=["suug"], channel="c")
        for a in range(DT):
            bk.op("vector", lambda v, a=a: v.tensor_scalar(
                out=h_su[:, a, :], in0=H128_bf[:], scalar1=su_ug_sb[:, a:a + 1],
                scalar2=None, op0=ALU.mult),
                reads=["H128", "suug"], writes=[f"hsu{a}"], tag="hsu")
        for a in range(DT):
            for tg in range(4):
                sl = tg % 2
                bank = 6 + sl
                bk.dma("gpsimd", lambda g, a=a, tg=tg, sl=sl: g.dma_start(
                    out=x1b[:, sl, :], in_=x1T[a * 128:(a + 1) * 128, tg * 512:(tg + 1) * 512]),
                    reads=["i"], writes=[f"x1b{sl}"], channel=f"xz{sl}", tag="ldx1")
                bk.op("tensor", lambda t, a=a, tg=tg, sl=sl, bank=bank: t.matmul(
                    pb[bank][:, 0:512], h_su[:, a, :], x1b[:, sl, :], start=True, stop=True),
                    reads=[f"hsu{a}", f"x1b{sl}"], writes=[f"pb{bank}"], tag="fw1")
                bk.op("scalar", lambda s, a=a, tg=tg, bank=bank: s.copy(
                    out=z_sb[:, a, tg * 512:(tg + 1) * 512], in_=pb[bank][:, 0:512]),
                    reads=[f"pb{bank}"], writes=["zsb"], tag="fw1e")
        emit_butterfly(nc, bk, z_sb[:], "zsb", DT, T, "z")

    bk.barrier()

    with nc.sbuf_tensor("wub", [128, 2, DT, 128], BF16) as wub, \
         nc.sbuf_tensor("wgb", [128, 2, DT, 128], BF16) as wgb, \
         nc.sbuf_tensor("wsu_sb", [128, 2, 64], F32) as wsu_sb, \
         nc.sbuf_tensor("gsil", [128, 512], F32) as gsil, \
         nc.sbuf_tensor("m_bf", [128, 2, 512], BF16) as m_bf:
        bk.dma("gpsimd", lambda g: g.dma_start(out=wsu_sb[:, 0, :], in_=io["Ws_up"]),
            reads=["i"], writes=["wsu"], channel="c")
        bk.dma("gpsimd", lambda g: g.dma_start(out=wsu_sb[:, 1, :], in_=io["Ws_gate"]),
            reads=["i"], writes=["wsu"], channel="c")
        bk.op("vector", lambda v: v.tensor_scalar(out=wsu_sb[:], in0=wsu_sb[:], scalar1=IS,
                                                  scalar2=None, op0=ALU.mult),
              reads=["wsu"], writes=["wsu"], tag="wsueff")
        for fb in range(64):
            sl = fb % 2
            bk.dma("gpsimd", lambda g, fb=fb, sl=sl: g.dma_start(
                out=wub[:, sl, :, :],
                in_=io["WupT"].rearrange("(a p) f -> p a f", p=128)[:, :, fb * 128:(fb + 1) * 128]),
                reads=["i"], writes=[f"wub{sl}"], channel=f"wu{sl}", tag="ldwu")
            bk.dma("gpsimd", lambda g, fb=fb, sl=sl: g.dma_start(
                out=wgb[:, sl, :, :],
                in_=io["WgateT"].rearrange("(a p) f -> p a f", p=128)[:, :, fb * 128:(fb + 1) * 128]),
                reads=["i"], writes=[f"wgb{sl}"], channel=f"wg{sl}", tag="ldwg")
            for tg in range(4):
                bu = 6 + (tg % 2)
                bg = 4 + (tg % 2)
                for a in range(DT):
                    bk.op("tensor", lambda t, a=a, fb=fb, tg=tg, sl=sl, bu=bu: t.matmul(
                        pb[bu][:, 0:512], wub[:, sl, a, :], z_sb[:, a, tg * 512:(tg + 1) * 512],
                        start=(a == 0), stop=(a == DT - 1)),
                        reads=[f"wub{sl}", "zsb"], writes=[f"pb{bu}"], tag="up")
                    bk.op("tensor", lambda t, a=a, fb=fb, tg=tg, sl=sl, bg=bg: t.matmul(
                        pb[bg][:, 0:512], wgb[:, sl, a, :], z_sb[:, a, tg * 512:(tg + 1) * 512],
                        start=(a == 0), stop=(a == DT - 1)),
                        reads=[f"wgb{sl}", "zsb"], writes=[f"pb{bg}"], tag="gate")
                bk.op("scalar", lambda s, fb=fb, bg=bg: s.activation(
                    out=gsil[:], in_=pb[bg][:, 0:512], func=AF.Silu, scale=wsu_sb[:, 1, fb:fb + 1]),
                    reads=[f"pb{bg}", "wsu"], writes=["gsil"], tag="silu")
                bk.op("vector", lambda v, fb=fb, tg=tg, bu=bu: v.scalar_tensor_tensor(
                    out=m_bf[:, tg % 2, :], in0=pb[bu][:, 0:512], scalar=wsu_sb[:, 0, fb:fb + 1],
                    in1=gsil[:], op0=ALU.mult, op1=ALU.mult),
                    reads=[f"pb{bu}", "wsu", "gsil"], writes=[f"mbf{tg % 2}"], tag="mtile")
                bk.dma("sync", lambda s, fb=fb, tg=tg: s.dma_start(
                    out=mT[fb * 128:(fb + 1) * 128, tg * 512:(tg + 1) * 512], in_=m_bf[:, tg % 2, :]),
                    reads=[f"mbf{tg % 2}"], writes=[f"mT{fb}{tg}"], channel=f"mw{tg % 2}", tag="stm")

    zctx.close()
    bk.barrier()

    with nc.sbuf_tensor("w_sb", [128, 64, 512], BF16) as w_sb, \
         nc.sbuf_tensor("su_dp_sb", [128, 64], F32) as su_dp_sb, \
         nc.sbuf_tensor("h_sud", [128, 64, 128], BF16) as h_sud, \
         nc.sbuf_tensor("wd_sb", [128, 2, 64, 128], BF16) as wd_sb, \
         nc.sbuf_tensor("x1pre", [128, DT, 512], F32) as x1pre, \
         nc.sbuf_tensor("xf_sb", [128, 2, 512], F32) as xf_sb, \
         nc.sbuf_tensor("pre_bf", [128, 2, 512], BF16) as pre_bf, \
         nc.sbuf_tensor("sq2_bf", [128, 2, 512], BF16) as sq2_bf, \
         nc.sbuf_tensor("st_row", [1, 4, 512], F32) as st_row, \
         nc.sbuf_tensor("x1f", [128, 2, 512], F32) as x1f:
        bk.dma("gpsimd", lambda g: g.dma_start(out=su_dp_sb[:], in_=io["SU_dp"]),
            reads=["i"], writes=["sudp"], channel="c")
        for b in range(64):
            bk.op("vector", lambda v, b=b: v.tensor_scalar(
                out=h_sud[:, b, :], in0=H128_bf[:], scalar1=su_dp_sb[:, b:b + 1],
                scalar2=None, op0=ALU.mult),
                reads=["H128", "sudp"], writes=[f"hsd{b}"], tag="hsud")

        def prep_j(J):
            bk.dma("gpsimd", lambda g, J=J: g.dma_start(
                out=w_sb[:], in_=mT.rearrange("(b p) t -> p b t", p=128)[:, :, J * 512:(J + 1) * 512]),
                reads=[f"mT{fb}{J}" for fb in range(64)], writes=["wsb"], channel="mr", tag="ldm")
            for b in range(64):
                bank = 6 + (b % 2)
                bk.op("tensor", lambda t, b=b, bank=bank: t.matmul(
                    pb[bank][:, 0:512], h_sud[:, b, :], w_sb[:, b, :], start=True, stop=True),
                    reads=[f"hsd{b}", "wsb"], writes=[f"pb{bank}"], tag="fw2")
                bk.op("scalar", lambda s, b=b, bank=bank: s.copy(
                    out=w_sb[:, b, :], in_=pb[bank][:, 0:512]),
                    reads=[f"pb{bank}"], writes=["wsb"], tag="fw2e")
            emit_butterfly(nc, bk, w_sb[:], "wsb", 64, 512, f"w{J}")

        def mk_matmul(bank, Dt, J):
            sl = Dt % 2
            bk.dma("gpsimd", lambda g, Dt=Dt, sl=sl: g.dma_start(
                out=wd_sb[:, sl, :, :],
                in_=io["WdownT"].rearrange("(b p) d -> p b d", p=128)[:, :, Dt * 128:(Dt + 1) * 128]),
                reads=["i"], writes=[f"wd{sl}"], channel=f"wd{sl}", tag="ldwd")
            for b in range(64):
                bk.op("tensor", lambda t, b=b, Dt=Dt, J=J, sl=sl, bank=bank: t.matmul(
                    pb[bank][:, 0:512], wd_sb[:, sl, b, :], w_sb[:, b, :],
                    start=(b == 0), stop=(b == 63)),
                    reads=[f"wd{sl}", "wsb"], writes=[f"pb{bank}"], tag="down")

        emit_postln(nc, bk, com, x1T, x2T, DT, mk_matmul,
                    (x1pre, xf_sb, pre_bf, sq2_bf, st_row, x1f),
                    prep_j=prep_j, scale_cols=lambda Dt: wsd_eff[:, Dt:Dt + 1], scale_res="wsd")


def host_constants_c():
    arr_row = np.arange(1, 257, dtype=np.float32).reshape(1, 256)
    rev_cst = np.zeros((128, 2), dtype=np.float32)
    for j in range(2):
        rev_cst[:, j] = 255.0 - (j * 128 + np.arange(128))
    n_idx = (np.arange(2)[:, None] * 128 + np.arange(128)[None, :])
    m_idx = np.arange(256)
    dmn = (m_idx[None, None, :] - n_idx[:, :, None]).astype(np.float32)
    dmask = (dmn >= 0).astype(np.float32)
    dmn = np.where(dmn >= 0, dmn, 0.0).astype(np.float32)
    dmn = np.ascontiguousarray(dmn.transpose(1, 0, 2))
    dmask = np.ascontiguousarray(dmask.transpose(1, 0, 2))
    ident = np.eye(128, dtype=np.float32)
    return dict(arr_row=arr_row, rev_cst=rev_cst, dmn=dmn, dmask=dmask, ident=ident)


def emit_l1_attn(nc, bk, ctx, com, io):
    from contextlib import ExitStack
    E = ctx.enter_context
    pb = com.pb
    x2T, x3T, obuf = io["x2T"], io["x3T"], io["obuf"]
    HN = 16
    pb6b = pb[6].bitcast(BF16)

    ssum_sb = E(nc.sbuf_tensor("ssum_sb", [1, T], F32))
    nw_sb = E(nc.sbuf_tensor("nw_sb", [128, 16], F32))
    bk.op("vector", lambda v: v.memset(ssum_sb[:], 0.0), writes=["ssum"])

    c2ctx = ExitStack()
    x2ctx = ExitStack()
    E2 = c2ctx.enter_context
    EX = x2ctx.enter_context
    x2Tb = EX(nc.sbuf_tensor("x2Tb", [128, DT, T], BF16))
    id_bf = E2(nc.sbuf_tensor("id_bf", [128, 128], BF16))
    dT_bf = E2(nc.sbuf_tensor("dT_bf", [128, HN, 2, 256], BF16))
    qdb = E2(nc.sbuf_tensor("qdb", [128, HN, 256], BF16))
    kd_col = E2(nc.sbuf_tensor("kd_col", [128, HN, 2], F32))
    bd_col = E2(nc.sbuf_tensor("bd_col", [128, HN], F32))
    srn_col = E2(nc.sbuf_tensor("srn_col", [128, HN], F32))
    v_all = E2(nc.sbuf_tensor("v_all", [128, DT, T], BF16))

    bk.dma("gpsimd", lambda g: g.dma_start(out=x2Tb[:], in_=x2T.rearrange("(a p) t -> p a t", p=128)),
           reads=["i"], writes=["x2Tb"], channel="x", tag="ldx2")
    bk.dma("gpsimd", lambda g: g.dma_start(out=id_bf[:], in_=io["ident"]),
           reads=["i"], writes=["idbf"], channel="c")
    bk.dma("gpsimd", lambda g: g.dma_start(out=nw_sb[:], in_=io["nw"]),
           reads=["i"], writes=["nw"], channel="c")

    with nc.sbuf_tensor("srow", [1, 2, HN], F32) as srow, \
         nc.sbuf_tensor("ac_sb", [1, 256], F32) as ac_sb, \
         nc.sbuf_tensor("rv_sb", [128, 2], F32) as rv_sb, \
         nc.sbuf_tensor("dm_sb", [128, 2, 256], F32) as dm_sb, \
         nc.sbuf_tensor("dk_bf", [128, 2, 256], BF16) as dk_bf, \
         nc.sbuf_tensor("qd_row", [1, HN, 256], F32) as qd_row:
        bk.dma("gpsimd", lambda g: g.dma_start(out=srow[0:1, 0, :], in_=io["sr"]),
               reads=["i"], writes=["srow"], channel="c")
        bk.dma("gpsimd", lambda g: g.dma_start(out=ac_sb[:], in_=io["arr_row"]),
               reads=["i"], writes=["ac"], channel="c")
        bk.dma("gpsimd", lambda g: g.dma_start(out=rv_sb[:], in_=io["rev_cst"]),
               reads=["i"], writes=["rv"], channel="c")
        bk.dma("gpsimd", lambda g: g.dma_start(out=dm_sb[:], in_=io["dmn"]),
               reads=["i"], writes=["dm"], channel="c")
        bk.dma("gpsimd", lambda g: g.dma_start(out=dk_bf[:], in_=io["dmask"]),
               reads=["i"], writes=["dk"], channel="c")
        bk.op("vector", lambda v: v.tensor_scalar(out=srow[0:1, 1, :], in0=srow[0:1, 0, :],
                                                  scalar1=-1.0, scalar2=None, op0=ALU.mult),
              reads=["srow"], writes=["srow"], tag="negs")
        com.bcast_row(bk, 5, srow[0:1, 1, :], "srow", HN, tag="sb")
        bk.op("scalar", lambda s: s.copy(out=srn_col[:], in_=pb[5][:, 0:HN]),
              reads=["pb5"], writes=["srn"], tag="srncp")
        bk.op("scalar", lambda s: s.activation(out=srow[0:1, 0, :], in_=srow[0:1, 1, :],
                                               func=AF.Exp, scale=256.0),
              reads=["srow"], writes=["srow"], tag="bdrow")
        com.bcast_row(bk, 5, srow[0:1, 0, :], "srow", HN, tag="bdb")
        bk.op("scalar", lambda s: s.copy(out=bd_col[:], in_=pb[5][:, 0:HN]),
              reads=["pb5"], writes=["bd"], tag="bdcp")
        for h in range(HN):
            bk.op("scalar", lambda s, h=h: s.activation(
                out=qd_row[0:1, h, :], in_=ac_sb[:], func=AF.Exp, scale=srow[0:1, 1, h:h + 1]),
                reads=["ac", "srow"], writes=["qdr"], tag="qd")
            bk.op("scalar", lambda s, h=h: s.activation(
                out=kd_col[:, h, :], in_=rv_sb[:], func=AF.Exp, scale=srn_col[:, h:h + 1]),
                reads=["rv", "srn"], writes=["kd"], tag="kd")
            bk.op("scalar", lambda s, h=h: s.activation(
                out=dT_bf[:, h, :, :], in_=dm_sb[:], func=AF.Exp, scale=srn_col[:, h:h + 1]),
                reads=["dm", "srn"], writes=[f"dT{h}"], tag="dexp")
            bk.op("vector", lambda v, h=h: v.tensor_tensor(
                out=dT_bf[:, h, :, :], in0=dT_bf[:, h, :, :], in1=dk_bf[:], op=ALU.mult),
                reads=[f"dT{h}", "dk"], writes=[f"dT{h}"], tag="dmask")
            com.bcast_row(bk, 5, qd_row[0:1, h, :], "qdr", 256, tag="qdbb")
            bk.op("scalar", lambda s, h=h: s.copy(out=qdb[:, h, :], in_=pb[5][:, 0:256]),
                  reads=["pb5"], writes=["qdb"], tag="qdbc")

    bk.barrier()

    with nc.sbuf_tensor("wv1", [128, 2, DT, 512], BF16) as wv1:
        for fg in range(4):
            sl = fg % 2
            bk.dma("gpsimd", lambda g, fg=fg, sl=sl: g.dma_start(
                out=wv1[:, sl, :, :],
                in_=io["WqkvT_v"].rearrange("(a p) f -> p a f", p=128)[:, :, fg * 512:(fg + 1) * 512]),
                reads=["i"], writes=[f"wv1{sl}"], channel=f"wv{sl}", tag="ldwv")
            for tt in range(DT):
                bank = 6 + (tt % 2)
                for a in range(DT):
                    bk.op("tensor", lambda t, a=a, tt=tt, fg=fg, sl=sl, bank=bank: t.matmul(
                        pb[bank][:, 0:512],
                        x2Tb[:, a, tt * 128:(tt + 1) * 128],
                        wv1[:, sl, a, :],
                        start=(a == 0), stop=(a == DT - 1)),
                        reads=[f"wv1{sl}", "x2Tb"], writes=[f"pb{bank}"], tag="vproj")
                bk.op("scalar", lambda s, tt=tt, fg=fg, bank=bank: s.activation(
                    out=v_all[:, tt, fg * 512:(fg + 1) * 512], in_=pb[bank][:, 0:512], func=AF.Silu),
                    reads=[f"pb{bank}"], writes=["vall"], tag="vsilu")

    bk.barrier()

    with nc.sbuf_tensor("wqk1", [128, 2, DT, 256], BF16) as wqk1, \
         nc.sbuf_tensor("qkT", [128, 2, T], BF16) as qkT, \
         nc.sbuf_tensor("kv_f", [128, 128], F32) as kv_f, \
         nc.sbuf_tensor("kv_bf", [128, 128], BF16) as kv_bf, \
         nc.sbuf_tensor("qq_bf", [128, 256], BF16) as qq_bf, \
         nc.sbuf_tensor("qks", [128, 2, 256], BF16) as qks, \
         nc.sbuf_tensor("qkd_bf", [128, 2, 256], BF16) as qkd_bf, \
         nc.sbuf_tensor("ktok", [128, 2, 128], BF16) as ktok, \
         nc.sbuf_tensor("och", [128, 256], BF16) as och, \
         nc.sbuf_tensor("osq", [128, 256], BF16) as osq:
        for h in range(HN):
            sl = h % 2
            bk.dma("gpsimd", lambda g, h=h, sl=sl: g.dma_start(
                out=wqk1[:, sl, :, :],
                in_=io["WqkvT_qk"].rearrange("(a p) f -> p a f", p=128)[:, :, h * 256:(h + 1) * 256]),
                reads=["i"], writes=[f"wqk{sl}"], channel=f"wq{sl}", tag="ldwqk")
            for which in range(2):
                for ng in range(4):
                    bank = 6 + (ng % 2)
                    for a in range(DT):
                        bk.op("tensor", lambda t, a=a, which=which, ng=ng, sl=sl, bank=bank: t.matmul(
                            pb[bank][:, 0:512],
                            wqk1[:, sl, a, which * 128:(which + 1) * 128],
                            x2Tb[:, a, ng * 512:(ng + 1) * 512],
                            start=(a == 0), stop=(a == DT - 1)),
                            reads=[f"wqk{sl}", "x2Tb"], writes=[f"pb{bank}"], tag="qkproj")
                    bk.op("scalar", lambda s, which=which, ng=ng, bank=bank: s.activation(
                        out=qkT[:, which, ng * 512:(ng + 1) * 512], in_=pb[bank][:, 0:512], func=AF.Silu),
                        reads=[f"pb{bank}"], writes=[f"qkT{which}"], tag="qksilu")
            bk.op("vector", lambda v: v.memset(kv_f[:], 0.0), writes=["kvf"], tag="kv0")
            bk.op("vector", lambda v: v.memset(kv_bf[:], 0.0), writes=["kvb"], tag="kv0b")
            for c in range(8):
                C0 = c * 256
                bk.op("vector", lambda v, h=h, C0=C0: v.tensor_tensor(
                    out=qq_bf[:], in0=qkT[:, 0, C0:C0 + 256], in1=qdb[:, h, :], op=ALU.mult),
                    reads=["qkT0", "qdb"], writes=["qq"], tag="qq")
                bk.op("tensor", lambda t: t.matmul(pb[2][:, 0:256], kv_bf[:], qq_bf[:],
                                                   start=True, stop=False),
                      reads=["kvb", "qq"], writes=["pb2"], tag="ond")
                for nh in range(2):
                    bk.op("tensor", lambda t, nh=nh, C0=C0: t.matmul(
                        pb[nh][:, 0:256],
                        qkT[:, 1, C0 + nh * 128:C0 + (nh + 1) * 128],
                        qkT[:, 0, C0:C0 + 256],
                        start=True, stop=True),
                        reads=["qkT1", "qkT0"], writes=[f"pb{nh}"], tag="qkt")
                    bk.op("scalar", lambda s, nh=nh: s.copy(out=qks[:, nh, :], in_=pb[nh][:, 0:256]),
                          reads=[f"pb{nh}"], writes=["qks"], tag="qkse")
                    bk.op("vector", lambda v, nh=nh, h=h: v.tensor_tensor(
                        out=qkd_bf[:, nh, :], in0=qks[:, nh, :], in1=dT_bf[:, h, nh, :], op=ALU.mult),
                        reads=["qks", f"dT{h}"], writes=["qkd"], tag="qkd")
                    bk.op("tensor", lambda t, nh=nh, c=c, h=h: t.matmul(
                        pb[2][:, 0:256], v_all[:, 2 * c + nh, h * 128:(h + 1) * 128], qkd_bf[:, nh, :],
                        start=False, stop=(nh == 1)),
                        reads=["vall", "qkd"], writes=["pb2"], tag="od")
                bk.op("scalar", lambda s: s.copy(out=och[:], in_=pb[2][:, 0:256]),
                      reads=["pb2"], writes=["och"], tag="oevac")
                bk.op("vector", lambda v: v.tensor_tensor(out=osq[:], in0=och[:], in1=och[:], op=ALU.mult),
                      reads=["och"], writes=["osq"], tag="osq")
                bk.op("tensor", lambda t: t.matmul(pb[4][0:1, 0:256], com.ones_col_bf[:], osq[:],
                                                   start=True, stop=True),
                      reads=["ones_col_bf", "osq"], writes=["pb4"], tag="osum")
                bk.op("vector", lambda v, C0=C0: v.tensor_tensor(
                    out=ssum_sb[0:1, C0:C0 + 256], in0=ssum_sb[0:1, C0:C0 + 256],
                    in1=pb[4][0:1, 0:256], op=ALU.add),
                    reads=["ssum", "pb4"], writes=["ssum"], tag="osacc")
                bk.dma("sync", lambda s, h=h, C0=C0: s.dma_start(
                    out=obuf[h * 128:(h + 1) * 128, C0:C0 + 256], in_=och[:]),
                    reads=["och"], writes=[f"ob{h}"], channel="ob", tag="stob")
                for nh in range(2):
                    bk.op("tensor", lambda t, nh=nh, C0=C0: t.transpose(
                        pb6b[:, 0:128], qkT[:, 1, C0 + nh * 128:C0 + (nh + 1) * 128], id_bf[:]),
                        reads=["qkT1", "idbf"], writes=["pb6"], tag="ktr")
                    bk.op("vector", lambda v, nh=nh, h=h: v.tensor_scalar(
                        out=ktok[:, nh, :], in0=pb6b[:, 0:128], scalar1=kd_col[:, h, nh:nh + 1],
                        scalar2=None, op0=ALU.mult),
                        reads=["pb6", "kd"], writes=["ktok"], tag="kdm")
                for nh in range(2):
                    bk.op("tensor", lambda t, nh=nh, c=c, h=h: t.matmul(
                        pb[3][:, 0:128], ktok[:, nh, :], v_all[:, 2 * c + nh, h * 128:(h + 1) * 128],
                        start=(nh == 0), stop=(nh == 1)),
                        reads=["ktok", "vall"], writes=["pb3"], tag="kvup")
                bk.op("vector", lambda v, h=h: v.scalar_tensor_tensor(
                    out=kv_f[:], in0=kv_f[:], scalar=bd_col[:, h:h + 1], in1=pb[3][:, 0:128],
                    op0=ALU.mult, op1=ALU.add),
                    reads=["kvf", "bd", "pb3"], writes=["kvf"], tag="kvnew")
                bk.op("vector", lambda v: v.tensor_copy(out=kv_bf[:], in_=kv_f[:]),
                      reads=["kvf"], writes=["kvb"], tag="kvcast")

    c2ctx.close()
    bk.barrier()

    gio = E(nc.sbuf_tensor("gio", [128, DT, T], BF16, side="right"))
    with nc.sbuf_tensor("wg1", [128, 2, DT, 128], BF16) as wg1, \
         nc.sbuf_tensor("rstd_bf", [128, 4, 512], BF16) as rstd_bf, \
         nc.sbuf_tensor("ob_sb", [128, T], BF16) as ob_sb, \
         nc.sbuf_tensor("gsig", [128, 512], BF16) as gsig, \
         nc.sbuf_tensor("t_bf", [128, 512], BF16) as t_bf:
        EPSF = float(np.finfo(np.float32).eps)
        bk.op("vector", lambda v: v.tensor_scalar(out=ssum_sb[:], in0=ssum_sb[:],
                                                  scalar1=1.0 / 2048.0, scalar2=EPSF,
                                                  op0=ALU.mult, op1=ALU.add),
              reads=["ssum"], writes=["ssum"], tag="meps")
        bk.op("scalar", lambda s: s.activation(out=ssum_sb[:], in_=ssum_sb[:], func=AF.Ln),
              reads=["ssum"], writes=["ssum"], tag="lnrms")
        bk.op("scalar", lambda s: s.activation(out=ssum_sb[:], in_=ssum_sb[:], func=AF.Exp, scale=-0.5),
              reads=["ssum"], writes=["ssum"], tag="exprms")
        for ng in range(4):
            com.bcast_row(bk, 5, ssum_sb[0:1, ng * 512:(ng + 1) * 512], "ssum", 512, tag="rmsb")
            bk.op("scalar", lambda s, ng=ng: s.copy(out=rstd_bf[:, ng, :], in_=pb[5][:, 0:512]),
                  reads=["pb5"], writes=["rstdbf"], tag="rmsbc")
        for h in range(DT):
            sl = h % 2
            bk.dma("gpsimd", lambda g, h=h, sl=sl: g.dma_start(
                out=wg1[:, sl, :, :],
                in_=io["WgT"].rearrange("(a p) f -> p a f", p=128)[:, :, h * 128:(h + 1) * 128]),
                reads=["i"], writes=[f"wg{sl}"], channel=f"wgl{sl}", tag="ldwg1")
            bk.dma("gpsimd", lambda g, h=h: g.dma_start(
                out=ob_sb[:], in_=obuf[h * 128:(h + 1) * 128, :]),
                reads=[f"ob{h}"], writes=["obsb"], channel="obr", tag="ldob")
            for ng in range(4):
                bank = 6 + (ng % 2)
                for a in range(DT):
                    bk.op("tensor", lambda t, a=a, h=h, ng=ng, sl=sl, bank=bank: t.matmul(
                        pb[bank][:, 0:512],
                        wg1[:, sl, a, :],
                        x2Tb[:, a, ng * 512:(ng + 1) * 512],
                        start=(a == 0), stop=(a == DT - 1)),
                        reads=[f"wg{sl}", "x2Tb"], writes=[f"pb{bank}"], tag="gproj")
                bk.op("scalar", lambda s, ng=ng, bank=bank: s.activation(
                    out=gsig[:], in_=pb[bank][:, 0:512], func=AF.Sigmoid),
                    reads=[f"pb{bank}"], writes=["gsig"], tag="gsig")
                bk.op("vector", lambda v, ng=ng: v.tensor_tensor(
                    out=t_bf[:], in0=ob_sb[:, ng * 512:(ng + 1) * 512], in1=rstd_bf[:, ng, :], op=ALU.mult),
                    reads=["obsb", "rstdbf"], writes=["tbf"], tag="onorm")
                bk.op("vector", lambda v, h=h: v.tensor_scalar(
                    out=t_bf[:], in0=t_bf[:], scalar1=nw_sb[:, h:h + 1], scalar2=None, op0=ALU.mult),
                    reads=["tbf", "nw"], writes=["tbf"], tag="onw")
                bk.op("vector", lambda v, h=h, ng=ng: v.tensor_tensor(
                    out=gio[:, h, ng * 512:(ng + 1) * 512], in0=t_bf[:], in1=gsig[:], op=ALU.mult),
                    reads=["tbf", "gsig"], writes=["gio"], tag="giom")

    x2ctx.close()
    bk.barrier()

    with nc.sbuf_tensor("wo1", [128, DT, D], BF16) as wo1, \
         nc.sbuf_tensor("x1pre", [128, DT, 512], F32) as x1pre, \
         nc.sbuf_tensor("xf_sb", [128, 2, 512], F32) as xf_sb, \
         nc.sbuf_tensor("pre_bf", [128, 2, 512], BF16) as pre_bf, \
         nc.sbuf_tensor("sq2_bf", [128, 2, 512], BF16) as sq2_bf, \
         nc.sbuf_tensor("st_row", [1, 4, 512], F32) as st_row, \
         nc.sbuf_tensor("x1f", [128, 2, 512], F32) as x1f:
        bk.dma("gpsimd", lambda g: g.dma_start(
            out=wo1[:], in_=io["WoT"].rearrange("(a p) f -> p a f", p=128)),
            reads=["i"], writes=["wo1"], channel="wo", tag="ldwo1")

        def mk_matmul(bank, Dt, J):
            for ft in range(DT):
                bk.op("tensor", lambda t, ft=ft, Dt=Dt, J=J, bank=bank: t.matmul(
                    pb[bank][:, 0:512],
                    wo1[:, ft, Dt * 128:(Dt + 1) * 128],
                    gio[:, ft, J * 512:(J + 1) * 512],
                    start=(ft == 0), stop=(ft == DT - 1)),
                    reads=["wo1", "gio"], writes=[f"pb{bank}"], tag="woL1")

        emit_postln(nc, bk, com, x2T, x3T, DT, mk_matmul,
                    (x1pre, xf_sb, pre_bf, sq2_bf, st_row, x1f))


def host_constants_d():
    q = np.arange(128)
    E8 = (q[:, None] // 16 == np.arange(8)[None, :]).astype(np.float32)
    TRI16 = ((q[:, None] // 16 == q[None, :] // 16) &
             (q[:, None] % 16 < q[None, :] % 16)).astype(np.float32)
    s = np.arange(8)
    jl = np.arange(128)
    tauc = (s[None, :, None] * 2048 + jl[None, None, :] * 16 +
            (q[:, None, None] % 16)).astype(np.float32).reshape(128, 1024)
    tloc = (jl[None, :] * 16 + (q[:, None] % 16)).astype(np.float32)
    wiota = np.arange(128, dtype=np.float32).reshape(128, 1)
    identf = np.eye(128, dtype=np.float32)
    E8T = np.ascontiguousarray(E8.T)
    iota4 = (np.arange(4)[None, :] * 128 + q[:, None]).astype(np.float32)
    iota16 = (np.arange(16)[None, :] * 128 + q[:, None]).astype(np.float32)
    iotah = (np.arange(128)[None, :] * 16 + np.arange(16)[:, None]).astype(np.float32)
    return dict(E8=E8, E8T=E8T, TRI16=TRI16, tauc=tauc, tloc=tloc, wiota=wiota,
                identf=identf, iota4=iota4, iota16=iota16, iotah=iotah)


def emit_router(nc, bk, ctx, com, io):
    pb = com.pb
    with nc.sbuf_tensor("x3b", [128, 2, 512], BF16) as x3b, \
         nc.sbuf_tensor("rw_sb", [128, DT, 8], BF16) as rw_sb, \
         nc.sbuf_tensor("rb_sb", [8, 1], F32) as rb_sb, \
         nc.sbuf_tensor("gmb", [8, T], F32) as gmb, \
         nc.sbuf_tensor("hT", [8, T], F32) as hT, \
         nc.sbuf_tensor("dnm", [1, T], F32) as dnm, \
         nc.sbuf_tensor("smT", [8, T], F32) as smT, \
         nc.sbuf_tensor("smp_sb", [8, T], F32) as smp_sb:
        bk.dma("gpsimd", lambda g: g.dma_start(
            out=rw_sb[:], in_=io["rw"].rearrange("(a p) e -> p a e", p=128)),
            reads=["i"], writes=["rw"], channel="c", tag="ldrw")
        bk.dma("gpsimd", lambda g: g.dma_start(out=rb_sb[:], in_=io["rb"]),
               reads=["i"], writes=["rb"], channel="c", tag="ldrb")
        bk.dma("gpsimd", lambda g: g.dma_start(out=gmb[:], in_=io["gumbelT"]),
               reads=["i"], writes=["gmb"], channel="c", tag="ldgm")
        for tg in range(4):
            bank = 6 + (tg % 2)
            for a in range(DT):
                sl = (tg * DT + a) % 2
                bk.dma("gpsimd", lambda g, a=a, tg=tg, sl=sl: g.dma_start(
                    out=x3b[:, sl, :], in_=io["x3T"][a * 128:(a + 1) * 128, tg * 512:(tg + 1) * 512]),
                    reads=["i"], writes=[f"x3b{sl}"], channel=f"x3{sl}", tag="ldx3")
                bk.op("tensor", lambda t, a=a, tg=tg, sl=sl, bank=bank: t.matmul(
                    pb[bank][0:8, 0:512], rw_sb[:, a, :], x3b[:, sl, :],
                    start=(a == 0), stop=(a == DT - 1)),
                    reads=["rw", f"x3b{sl}"], writes=[f"pb{bank}"], tag="router")
            bk.op("vector", lambda v, tg=tg, bank=bank: v.tensor_scalar(
                out=hT[:, tg * 512:(tg + 1) * 512], in0=pb[bank][0:8, 0:512], scalar1=rb_sb[:],
                scalar2=None, op0=ALU.add),
                reads=[f"pb{bank}", "rb"], writes=["hT"], tag="addrb")
        bk.op("vector", lambda v: v.tensor_tensor(out=hT[:], in0=hT[:], in1=gmb[:], op=ALU.add),
              reads=["hT", "gmb"], writes=["hT"], tag="addgm")
        bk.op("scalar", lambda s: s.activation(out=hT[:], in_=hT[:], func=AF.Exp),
              reads=["hT"], writes=["hT"], tag="rexp")
        for tg in range(4):
            bk.op("tensor", lambda t, tg=tg: t.matmul(
                pb[5][0:1, 0:512], com.ones_col_f[0:8, :], hT[:, tg * 512:(tg + 1) * 512],
                start=True, stop=True),
                reads=["ones_col_f", "hT"], writes=["pb5"], tag="rden")
            bk.op("scalar", lambda s, tg=tg: s.copy(out=dnm[0:1, tg * 512:(tg + 1) * 512],
                                                    in_=pb[5][0:1, 0:512]),
                  reads=["pb5"], writes=["dnm"], tag="rdencp")
        bk.op("scalar", lambda s: s.activation(out=dnm[:], in_=dnm[:], func=AF.Ln),
              reads=["dnm"], writes=["dnm"], tag="rdln")
        bk.op("scalar", lambda s: s.activation(out=dnm[:], in_=dnm[:], func=AF.Exp, scale=-1.0),
              reads=["dnm"], writes=["dnm"], tag="rdexp")
        for tg in range(4):
            com.bcast_row(bk, 5, dnm[0:1, tg * 512:(tg + 1) * 512], "dnm", 512, tag="rdb")
            bk.op("vector", lambda v, tg=tg: v.tensor_tensor(
                out=smT[:, tg * 512:(tg + 1) * 512], in0=hT[:, tg * 512:(tg + 1) * 512],
                in1=pb[5][0:8, 0:512], op=ALU.mult),
                reads=["hT", "pb5"], writes=["smT"], tag="rsm")
        bk.op("vector", lambda v: v.tensor_copy(
            out=smp_sb[:].rearrange("e (p2 jl) -> e p2 jl", jl=128),
            in_=smT[:].rearrange("e (jl p2) -> e p2 jl", p2=16)),
            reads=["smT"], writes=["smperm"], tag="rperm")
        bk.dma("sync", lambda s: s.dma_start(out=io["smperm"], in_=smp_sb[:]),
               reads=["smperm"], writes=["o:smperm"], channel="o", tag="stsm")


def emit_bisect(nc, bk, ctx, com, io, B_sb, theta_col, cnt_ps, msk1, niter=30):
    pb = com.pb
    E8, E8T = io["__E8sb"], None
    with nc.sbuf_tensor("cnt1", [128, 1], F32) as cnt1, \
         nc.sbuf_tensor("cntc", [128, 1], F32) as cntc, \
         nc.sbuf_tensor("crow", [1, 8], F32) as crow, \
         nc.sbuf_tensor("sg8", [128, 1], F32) as sg8, \
         nc.sbuf_tensor("dl8", [128, 1], F32) as dl8, \
         nc.sbuf_tensor("tmp8", [128, 8], F32) as tmp8:
        bk.dma("gpsimd", lambda g: g.dma_start(
            out=B_sb[:], in_=io["smfp"].rearrange("(s e) (p j) -> (e p) s j", e=8, p=16)),
            reads=["i:smfp"], writes=["B"], channel="bt", tag="ldB")
        bk.op("vector", lambda v: v.memset(theta_col[:], 0.5), writes=["theta"], tag="th0")
        bk.op("vector", lambda v: v.memset(dl8[:], 0.25), writes=["dl8"], tag="dl0")
        for it in range(niter):
            bk.op("vector", lambda v: v.tensor_scalar(
                out=msk1[:], in0=B_sb[:].rearrange("p a b -> p (a b)"), scalar1=theta_col[:],
                scalar2=None, op0=ALU.is_gt, op1=ALU.add, accum_out=cnt1[:]),
                reads=["B", "theta"], writes=["msk1", "cnt1"], tag=f"bi{it}")
            bk.op("tensor", lambda t: t.matmul(pb[5][0:1, 0:8], cnt1[:], E8[:],
                                               start=True, stop=True),
                  reads=["cnt1", "E8"], writes=["pb5"], tag="bcnt")
            bk.op("scalar", lambda s: s.copy(out=crow[:], in_=pb[5][0:1, 0:8]),
                  reads=["pb5"], writes=["crow"], tag="bcrow")
            com.bcast_row(bk, 5, crow[0:1, :], "crow", 8, tag="bcb")
            bk.op("vector", lambda v: v.tensor_tensor(out=tmp8[:], in0=pb[5][:, 0:8], in1=E8[:], op=ALU.mult),
                  reads=["pb5", "E8"], writes=["tmp8"], tag="bsel")
            bk.op("vector", lambda v: v.tensor_reduce(out=cntc[:], in_=tmp8[:], axis=mybir.AxisListType.X,
                                                      op=ALU.add),
                  reads=["tmp8"], writes=["cntc"], tag="bred")
            bk.op("vector", lambda v: v.tensor_scalar(out=cntc[:], in0=cntc[:], scalar1=-2048.0,
                                                      scalar2=None, op0=ALU.add),
                  reads=["cntc"], writes=["cntc"], tag="bsub")
            bk.op("scalar", lambda s: s.activation(out=sg8[:], in_=cntc[:], func=AF.Sign),
                  reads=["cntc"], writes=["sg8"], tag="bsign")
            bk.op("vector", lambda v: v.scalar_tensor_tensor(
                out=theta_col[:], in0=sg8[:], scalar=dl8[:], in1=theta_col[:],
                op0=ALU.mult, op1=ALU.add),
                reads=["sg8", "dl8", "theta"], writes=["theta"], tag="bup")
            bk.op("vector", lambda v: v.tensor_scalar(out=dl8[:], in0=dl8[:], scalar1=0.5,
                                                      scalar2=None, op0=ALU.mult),
                  reads=["dl8"], writes=["dl8"], tag="bhalf")
        for sh in range(8):
            bk.op("vector", lambda v, sh=sh: v.tensor_scalar(
                out=msk1[:, sh * 128:(sh + 1) * 128], in0=B_sb[:, sh, :], scalar1=theta_col[:],
                scalar2=None, op0=ALU.is_gt, op1=ALU.add, accum_out=cnt_ps[:, sh:sh + 1]),
                reads=["B", "theta"], writes=["msk1", "cntps"], tag=f"fin{sh}")
    bk.barrier()


DFFP = 5504
NB_DFF = 43


def emit_expert(nc, bk, ctx, com, io):
    from contextlib import ExitStack
    E = ctx.enter_context
    pb = com.pb

    E8sb = E(nc.sbuf_tensor("E8sb", [128, 8], F32))
    bk.dma("gpsimd", lambda g: g.dma_start(out=E8sb[:], in_=io["E8"]),
           reads=["i"], writes=["E8"], channel="c", tag="ldE8")
    io["__E8sb"] = E8sb
    IDXu = E(nc.sbuf_tensor("IDXu", [128, 16], U32))
    Gf = E(nc.sbuf_tensor("Gf", [128, 16], F32))

    dctx = ExitStack()
    ED = dctx.enter_context
    B_sb = ED(nc.sbuf_tensor("B_sb", [128, 8, 128], F32))
    theta = ED(nc.sbuf_tensor("theta", [128, 1], F32))
    cnt_ps = ED(nc.sbuf_tensor("cnt_ps", [128, 8], F32))
    msk1 = ED(nc.sbuf_tensor("msk1", [128, 1024], F32))
    tauc_sb = ED(nc.sbuf_tensor("tauc_sb", [128, 1024], F32))
    sel_sb = ED(nc.sbuf_tensor("sel_sb", [128, 1024], F32))
    selc = ED(nc.sbuf_tensor("selc", [16, 1024], F32))
    selcG = ED(nc.sbuf_tensor("selcG", [16, 1024], F32))
    rowsel_sb = ED(nc.sbuf_tensor("rowsel_sb", [16, 1], U32))
    sgo = ED(nc.sbuf_tensor("sgo", [16, 128], F32))
    sgh = ED(nc.sbuf_tensor("sgh", [16, 2, 128], F32))
    nf_sb = ED(nc.sbuf_tensor("nf_sb", [1, 1], U32))
    idflt = ED(nc.sbuf_tensor("idflt", [128, 16], F32))
    Rload = ED(nc.sbuf_tensor("Rload", [16, 128], F32))
    IDXf = ED(nc.sbuf_tensor("IDXf", [128, 16], F32))
    idf32 = ED(nc.sbuf_tensor("idf32", [16, 16], F32))
    iotah_sb = ED(nc.sbuf_tensor("iotah_sb", [16, 128], F32))
    iota16_sb = ED(nc.sbuf_tensor("iota16_sb", [128, 16], F32))
    neg1h = ED(nc.sbuf_tensor("neg1h", [16, 128], F32))
    zer16 = ED(nc.sbuf_tensor("zer16", [128, 16], F32))
    big16 = ED(nc.sbuf_tensor("big16", [128, 16], F32))
    cscr = ED(nc.sbuf_tensor("cscr", [16, 512], F32))
    cnt16 = ED(nc.sbuf_tensor("cnt16", [16, 2], F32))
    crow2 = ED(nc.sbuf_tensor("crow2", [1, 4], F32))
    ch16 = ED(nc.sbuf_tensor("ch16", [16, 1], F32))
    invh = ED(nc.sbuf_tensor("invh", [16, 128], F32))
    invh_u8 = ED(nc.sbuf_tensor("invh_u8", [16, 128], U32))
    inv16 = ED(nc.sbuf_tensor("inv16", [128, 16], F32))
    inv16_u8 = ED(nc.sbuf_tensor("inv16_u8", [128, 16], U32))
    ctcol = ED(nc.sbuf_tensor("ctcol", [128, 1], F32))
    taur = ED(nc.sbuf_tensor("taur", [128, 16, 16], F32))

    emit_bisect(nc, bk, ctx, com, io, B_sb, theta, cnt_ps, msk1)
    bk.dma("gpsimd", lambda g: g.dma_start(out=tauc_sb[:], in_=io["tauc"]),
           reads=["i"], writes=["tauc"], channel="c", tag="ldtau")
    bk.dma("gpsimd", lambda g: g.dma_start(out=idf32[:], in_=io["identf"][0:16, 0:16]),
           reads=["i"], writes=["idf"], channel="c", tag="ldidf")
    bk.dma("gpsimd", lambda g: g.dma_start(out=iotah_sb[:], in_=io["iotah"]),
           reads=["i"], writes=["iotah"], channel="c", tag="ldioh")
    bk.dma("gpsimd", lambda g: g.dma_start(out=iota16_sb[:], in_=io["iota16"]),
           reads=["i"], writes=["iota16"], channel="c", tag="ldio16")
    bk.op("vector", lambda v: v.memset(neg1h[:], -1.0), writes=["neg1h"], tag="n1h")
    bk.op("vector", lambda v: v.memset(zer16[:], 0.0), writes=["zer16"], tag="z16")
    bk.op("vector", lambda v: v.memset(big16[:], 65536.0), writes=["big16"], tag="b16")
    bk.op("vector", lambda v: v.scalar_tensor_tensor(
        out=sel_sb[:], in0=tauc_sb[:], scalar=1.0, in1=msk1[:], op0=ALU.add, op1=ALU.mult),
        reads=["tauc", "msk1"], writes=["sel"], tag="sel1")
    bk.op("vector", lambda v: v.tensor_scalar(out=sel_sb[:], in0=sel_sb[:], scalar1=-1.0,
                                              scalar2=None, op0=ALU.add),
          reads=["sel"], writes=["sel"], tag="sel2")
    bk.dma("sync", lambda s: s.dma_start(out=io["scr"][0:128, :], in_=sel_sb[:]),
           reads=["sel"], writes=["scr:T"], channel="sc", tag="stsel")
    bk.op("vector", lambda v: v.scalar_tensor_tensor(
        out=sel_sb[:], in0=B_sb[:].rearrange("p a b -> p (a b)"), scalar=1.0, in1=msk1[:],
        op0=ALU.add, op1=ALU.mult),
        reads=["B", "msk1", "sel"], writes=["sel"], tag="selg1")
    bk.op("vector", lambda v: v.tensor_scalar(out=sel_sb[:], in0=sel_sb[:], scalar1=-1.0,
                                              scalar2=None, op0=ALU.add),
          reads=["sel"], writes=["sel"], tag="selg2")
    bk.dma("sync", lambda s: s.dma_start(out=io["scr2"][0:128, :], in_=sel_sb[:]),
           reads=["sel"], writes=["scr:G"], channel="sc", tag="stselg")
    bk.dma("gpsimd", lambda g: g.dma_start(out=rowsel_sb[:], in_=io["rowsel"]),
           reads=["i"], writes=["rowsel"], channel="c", tag="ldrs")
    bk.dma("gpsimd", lambda g: g.indirect_dma_start(
        out=selc[:], out_offset=None, in_=io["scr"], in_offset=IOA(rowsel_sb[:], 0)),
        reads=["scr:T", "rowsel"], writes=["selc"], channel="sc2", tag="ldselc")
    bk.dma("gpsimd", lambda g: g.indirect_dma_start(
        out=selcG[:], out_offset=None, in_=io["scr2"], in_offset=IOA(rowsel_sb[:], 0)),
        reads=["scr:G", "rowsel"], writes=["selcG"], channel="sc2", tag="ldselcG")
    for hf in range(2):
        bk.op("vector", lambda v, hf=hf: v.tensor_scalar(
            out=cscr[:], in0=selc[:, hf * 512:(hf + 1) * 512], scalar1=0.0, scalar2=None,
            op0=ALU.is_ge, op1=ALU.add, accum_out=cnt16[:, hf:hf + 1]),
            reads=["selc"], writes=["cscr", "cnt16"], tag=f"chalf{hf}")
    for hf in range(2):
        bk.op("tensor", lambda t, hf=hf: t.matmul(pb[5][0:1, hf:hf + 1], com.ones_col_f[0:16, :],
                                                  cnt16[:, hf:hf + 1], start=True, stop=True),
              reads=["ones_col_f", "cnt16"], writes=["pb5"], tag=f"chsum{hf}")
        bk.op("scalar", lambda s, hf=hf: s.copy(out=crow2[0:1, hf:hf + 1], in_=pb[5][0:1, hf:hf + 1]),
              reads=["pb5"], writes=["crow2"], tag=f"chcp{hf}")
    bk.op("vector", lambda v: v.tensor_tensor(out=crow2[0:1, 2:3], in0=crow2[0:1, 0:1],
                                              in1=crow2[0:1, 1:2], op=ALU.add),
          reads=["crow2"], writes=["crow2"], tag="ctot")
    com.bcast_row(bk, 5, crow2[0:1, 2:3], "crow2", 1, tag="ctb")
    bk.op("scalar", lambda s: s.copy(out=ctcol[:], in_=pb[5][:, 0:1]),
          reads=["pb5"], writes=["ctcol"], tag="ctbe")
    bk.op("vector", lambda v: v.tensor_scalar(
        out=inv16[:], in0=iota16_sb[:], scalar1=ctcol[:], scalar2=None, op0=ALU.is_ge),
        reads=["iota16", "ctcol"], writes=["inv16"], tag="invt")
    bk.op("vector", lambda v: v.tensor_copy(out=inv16_u8[:], in_=inv16[:]),
          reads=["inv16"], writes=["inv16u"], tag="invtu")

    def compact(sel_ap, sel_res, out_flat_res, dst_sb, dst_res, pad_sb, pad_res, tagp):
        for hf in range(2):
            bk.op("gpsimd", lambda g, hf=hf: g.sparse_gather(
                out=sgh[:, hf, :], in_=sel_ap[:, hf * 512:(hf + 1) * 512], num_found=nf_sb[:]),
                reads=[sel_res, "sgh"], writes=["sgh", "nf"], tag=f"{tagp}sg{hf}")
            com.bcast_row(bk, 5, crow2[0:1, hf:hf + 1], "crow2", 1, tag=f"{tagp}chb{hf}")
            bk.op("scalar", lambda s, hf=hf: s.copy(out=ch16[:], in_=pb[5][0:16, 0:1]),
                  reads=["pb5"], writes=["ch16"], tag=f"{tagp}chbe{hf}")
            bk.op("vector", lambda v, hf=hf: v.tensor_scalar(
                out=invh[:], in0=iotah_sb[:], scalar1=ch16[:], scalar2=None, op0=ALU.is_ge),
                reads=["iotah", "ch16"], writes=["invh"], tag=f"{tagp}inv{hf}")
            bk.op("vector", lambda v, hf=hf: v.tensor_copy(out=invh_u8[:], in_=invh[:]),
                  reads=["invh"], writes=["invhu"], tag=f"{tagp}invu{hf}")
            bk.op("vector", lambda v, hf=hf: v.copy_predicated(out=sgh[:, hf, :], mask=invh_u8[:], data=neg1h[:]),
                  reads=["invhu", "neg1h", "sgh"], writes=["sgh"], tag=f"{tagp}cl{hf}")
        bk.op("gpsimd", lambda g: g.sparse_gather(out=sgo[:], in_=sgh[:].rearrange("p a b -> p (a b)"),
                                                  num_found=nf_sb[:]),
              reads=["sgh", "sgo"], writes=["sgo", "nf"], tag=f"{tagp}sgf")
        bk.op("tensor", lambda t: t.transpose(pb[6][:, 0:16], sgo[:], idf32[:]),
              reads=["sgo", "idf"], writes=["pb6"], tag=f"{tagp}t1")
        bk.op("scalar", lambda s: s.copy(out=idflt[:], in_=pb[6][:, 0:16]),
              reads=["pb6"], writes=["idflt"], tag=f"{tagp}t1e")
        bk.dma("sync", lambda s: s.dma_start(
            out=io["scrF2"].rearrange("a b -> (a b)").rearrange("(q i) -> q i", q=128),
            in_=idflt[:]),
            reads=["idflt"], writes=["scrF2"], channel="sc", tag=f"{tagp}stT1")
        bk.dma("gpsimd", lambda g: g.dma_start(
            out=Rload[:], in_=io["scrF2"].rearrange("a b -> (a b)").rearrange("(a b) -> a b", a=16)),
            reads=["scrF2"], writes=["Rload"], channel="sc2", tag=f"{tagp}ldR")
        bk.op("tensor", lambda t: t.transpose(pb[6][:, 0:16], Rload[:], idf32[:]),
              reads=["Rload", "idf"], writes=["pb6"], tag=f"{tagp}t2")
        bk.op("scalar", lambda s: s.copy(out=dst_sb[:], in_=pb[6][:, 0:16]),
              reads=["pb6"], writes=[dst_res], tag=f"{tagp}t2e")
        bk.op("vector", lambda v: v.copy_predicated(out=dst_sb[:], mask=inv16_u8[:], data=pad_sb[:]),
              reads=["inv16u", pad_res, dst_res], writes=[dst_res], tag=f"{tagp}clean")

    compact(selc, "selc", "fT", IDXf, "IDXf", zer16, "zer16", "cT")
    bk.op("vector", lambda v: v.tensor_copy(out=IDXu[:], in_=IDXf[:]),
          reads=["IDXf"], writes=["IDXu"], tag="idxcast")
    compact(selcG, "selcG", "fG", Gf, "Gf", zer16, "zer16", "cG")
    bk.op("vector", lambda v: v.copy_predicated(out=IDXf[:], mask=inv16_u8[:], data=big16[:]),
          reads=["inv16u", "big16", "IDXf"], writes=["IDXf"], tag="taupad")
    for w in range(16):
        bk.op("vector", lambda v, w=w: v.tensor_copy(out=taur[:, :, w], in_=IDXf[:]),
              reads=["IDXf"], writes=["taur"], tag=f"taur{w}")
    bk.dma("sync", lambda s: s.dma_start(
        out=io["taub"].rearrange("(i q) w -> q i w", q=128), in_=taur[:]),
        reads=["taur"], writes=["o:taub"], channel="sc", tag="sttaub")
    dctx.close()
    bk.barrier()

    x2ctx = ExitStack()
    E2 = x2ctx.enter_context
    Xg = E2(nc.sbuf_tensor("Xg", [128, DT, T], BF16))
    XTg = E(nc.sbuf_tensor("XTg", [128, DT, T], BF16, side="right"))
    idb2 = E2(nc.sbuf_tensor("idb2", [128, 128], BF16))
    bk.dma("gpsimd", lambda g: g.dma_start(out=idb2[:], in_=io["identf"]),
           reads=["i"], writes=["idb2"], channel="c", tag="ldid2")
    pb6b = pb[6].bitcast(BF16)
    pb7b = pb[7].bitcast(BF16)
    for i in range(16):
        bk.dma("gpsimd", lambda g, i=i: g.indirect_dma_start(
            out=Xg[:, i, :], out_offset=None, in_=io["x3tok"],
            in_offset=IOA(IDXu[:, i:i + 1], 0)),
            reads=["IDXu", "i:x3tok"], writes=[f"Xg{i}"], channel="gx", tag=f"gx{i}")
    for st in range(16):
        for dt in range(DT):
            bnk = 6 + (dt % 2)
            pbx = pb6b if bnk == 6 else pb7b
            bk.op("tensor", lambda t, st=st, dt=dt, pbx=pbx: t.transpose(
                pbx[:, 0:128], Xg[:, st, dt * 128:(dt + 1) * 128], idb2[:]),
                reads=[f"Xg{st}", "idb2"], writes=[f"pb{bnk}"], tag="xtr")
            bk.op("scalar", lambda s, st=st, dt=dt, bnk=bnk, pbx=pbx: s.copy(
                out=XTg[:, dt, st * 128:(st + 1) * 128], in_=pbx[:, 0:128]),
                reads=[f"pb{bnk}"], writes=["XTg"], tag="xtre")
    x2ctx.close()
    bk.barrier()

    with nc.sbuf_tensor("wgx", [128, 2, DT, 128], BF16) as wgx, \
         nc.sbuf_tensor("wux", [128, 2, DT, 128], BF16) as wux, \
         nc.sbuf_tensor("bg_sb", [128, NB_DFF], F32) as bg_sb, \
         nc.sbuf_tensor("bu_sb", [128, NB_DFF], F32) as bu_sb, \
         nc.sbuf_tensor("gsil", [128, 512], F32) as gsil, \
         nc.sbuf_tensor("m_bf", [128, 2, 512], BF16) as m_bf:
        bk.dma("gpsimd", lambda g: g.dma_start(out=bg_sb[:], in_=io["bgate"]),
               reads=["i"], writes=["bg"], channel="c", tag="ldbg")
        bk.dma("gpsimd", lambda g: g.dma_start(out=bu_sb[:], in_=io["bup"]),
               reads=["i"], writes=["bu"], channel="c", tag="ldbu")
        for fb in range(NB_DFF):
            sl = fb % 2
            bk.dma("gpsimd", lambda g, fb=fb, sl=sl: g.dma_start(
                out=wgx[:, sl, :, :],
                in_=io["wgate"].rearrange("(a p) f -> p a f", p=128)[:, :, fb * 128:(fb + 1) * 128]),
                reads=["i"], writes=[f"wgx{sl}"], channel=f"eg{sl}", tag="ldwgx")
            bk.dma("gpsimd", lambda g, fb=fb, sl=sl: g.dma_start(
                out=wux[:, sl, :, :],
                in_=io["wup"].rearrange("(a p) f -> p a f", p=128)[:, :, fb * 128:(fb + 1) * 128]),
                reads=["i"], writes=[f"wux{sl}"], channel=f"eu{sl}", tag="ldwux")
            for tg in range(4):
                bg_ = 4 + (tg % 2)
                bu_ = 6 + (tg % 2)
                for a in range(DT):
                    bk.op("tensor", lambda t, a=a, tg=tg, sl=sl, bg_=bg_: t.matmul(
                        pb[bg_][:, 0:512], wgx[:, sl, a, :],
                        XTg[:, a, tg * 512:(tg + 1) * 512],
                        start=(a == 0), stop=(a == DT - 1)),
                        reads=[f"wgx{sl}", "XTg"], writes=[f"pb{bg_}"], tag="eg")
                    bk.op("tensor", lambda t, a=a, tg=tg, sl=sl, bu_=bu_: t.matmul(
                        pb[bu_][:, 0:512], wux[:, sl, a, :],
                        XTg[:, a, tg * 512:(tg + 1) * 512],
                        start=(a == 0), stop=(a == DT - 1)),
                        reads=[f"wux{sl}", "XTg"], writes=[f"pb{bu_}"], tag="eu")
                bk.op("scalar", lambda s, fb=fb, bg_=bg_: s.activation(
                    out=gsil[:], in_=pb[bg_][:, 0:512], func=AF.Silu, bias=bg_sb[:, fb:fb + 1]),
                    reads=[f"pb{bg_}", "bg"], writes=["gsil"], tag="esilu")
                bk.op("vector", lambda v, fb=fb, tg=tg, bu_=bu_: v.scalar_tensor_tensor(
                    out=m_bf[:, tg % 2, :], in0=pb[bu_][:, 0:512],
                    scalar=bu_sb[:, fb:fb + 1], in1=gsil[:], op0=ALU.add, op1=ALU.mult),
                    reads=[f"pb{bu_}", "bu", "gsil"], writes=[f"mbf{tg % 2}"], tag="emt")
                bk.dma("sync", lambda s, fb=fb, tg=tg: s.dma_start(
                    out=io["me"][fb * 128:(fb + 1) * 128, tg * 512:(tg + 1) * 512],
                    in_=m_bf[:, tg % 2, :]),
                    reads=[f"mbf{tg % 2}"], writes=[f"me{fb}{tg}"], channel=f"mw{tg % 2}", tag="stme")

    bk.barrier()

    with nc.sbuf_tensor("wdx", [128, NB_DFF, 1024], BF16) as wdx, \
         nc.sbuf_tensor("m_st", [128, 2, NB_DFF, 128], BF16) as m_st, \
         nc.sbuf_tensor("bdb", [128, 4, 512], F32) as bdb, \
         nc.sbuf_tensor("bdrow", [1, 2048], F32) as bdrow, \
         nc.sbuf_tensor("otf", [128, 2, 512], F32) as otf, \
         nc.sbuf_tensor("ot_bf", [128, 2, 512], BF16) as ot_bf:
        bk.dma("gpsimd", lambda g: g.dma_start(out=bdrow[:], in_=io["bdown"]),
               reads=["i"], writes=["bdr"], channel="c", tag="ldbd")
        for Dg in range(4):
            com.bcast_row(bk, 5, bdrow[0:1, Dg * 512:(Dg + 1) * 512], "bdr", 512, tag="bdbc")
            bk.op("scalar", lambda s, Dg=Dg: s.copy(out=bdb[:, Dg, :], in_=pb[5][:, 0:512]),
                  reads=["pb5"], writes=["bdbuf"], tag="bdbe")
        for wh in range(2):
            bk.dma("gpsimd", lambda g, wh=wh: g.dma_start(
                out=wdx[:], in_=io["wdown"].rearrange("(b p) d -> p b d", p=128)[:, :, wh * 1024:(wh + 1) * 1024]),
                reads=["i"], writes=["wdx"], channel="ed", tag="ldwdx")
            for st in range(16):
                msl = st % 2
                bk.dma("gpsimd", lambda g, st=st, msl=msl: g.dma_start(
                    out=m_st[:, msl, :, :],
                    in_=io["me"].rearrange("(b p) t -> p b t", p=128)[:, :, st * 128:(st + 1) * 128]),
                    reads=[f"me{fb}{st // 4}" for fb in range(NB_DFF)],
                    writes=[f"mst{msl}"], channel=f"mr{msl}", tag="ldmst")
                for Dg2 in range(2):
                    bnk = 6 + (Dg2 % 2)
                    for b in range(NB_DFF):
                        bk.op("tensor", lambda t, b=b, st=st, Dg2=Dg2, msl=msl, bnk=bnk: t.matmul(
                            pb[bnk][:, 0:512], m_st[:, msl, b, :], wdx[:, b, Dg2 * 512:(Dg2 + 1) * 512],
                            start=(b == 0), stop=(b == NB_DFF - 1)),
                            reads=[f"mst{msl}", "wdx"], writes=[f"pb{bnk}"], tag="ed")
                    Dgg = wh * 2 + Dg2
                    bk.op("vector", lambda v, st=st, Dgg=Dgg, bnk=bnk: v.tensor_tensor(
                        out=otf[:, st % 2, :], in0=pb[bnk][:, 0:512], in1=bdb[:, Dgg, :], op=ALU.add),
                        reads=[f"pb{bnk}", "bdbuf"], writes=[f"otf{st % 2}"], tag="eout1")
                    bk.op("vector", lambda v, st=st, Dgg=Dgg: v.tensor_scalar(
                        out=ot_bf[:, st % 2, :], in0=otf[:, st % 2, :], scalar1=Gf[:, st:st + 1],
                        scalar2=None, op0=ALU.mult),
                        reads=[f"otf{st % 2}", "Gf"], writes=[f"ot{st % 2}"], tag="eout2")
                    bk.dma("sync", lambda s, st=st, Dgg=Dgg: s.dma_start(
                        out=io["OUT"][st * 128:(st + 1) * 128, Dgg * 512:(Dgg + 1) * 512],
                        in_=ot_bf[:, st % 2, :]),
                        reads=[f"ot{st % 2}"], writes=["o:OUT"], channel=f"oo{st % 2}", tag="stout")


def emit_scatter_ln(nc, bk, ctx, com, io):
    from contextlib import ExitStack
    E = ctx.enter_context
    pb = com.pb
    E8sb = E(nc.sbuf_tensor("E8sb", [128, 8], F32))
    bk.dma("gpsimd", lambda g: g.dma_start(out=E8sb[:], in_=io["E8"]),
           reads=["i"], writes=["E8"], channel="c", tag="ldE8")
    io["__E8sb"] = E8sb
    rbase = E(nc.sbuf_tensor("rbase_sb", [128, 1], F32))
    bk.dma("gpsimd", lambda g: g.dma_start(out=rbase[:], in_=io["rbase"]),
           reads=["i"], writes=["rbase"], channel="c", tag="ldrb2")

    B_sb = E(nc.sbuf_tensor("B_sb", [128, 8, 128], F32))
    theta = E(nc.sbuf_tensor("theta", [128, 1], F32))
    cnt_ps = E(nc.sbuf_tensor("cnt_ps", [128, 8], F32))
    msk1 = E(nc.sbuf_tensor("msk1", [128, 1024], F32))
    iota4_sb = E(nc.sbuf_tensor("iota4_sb", [128, 4], F32))
    rsel8 = E(nc.sbuf_tensor("rsel8_sb", [8, 8], F32))
    po_col = E(nc.sbuf_tensor("po_col", [8, 1], F32))
    bk.dma("gpsimd", lambda g: g.dma_start(out=rsel8[:], in_=io["rsel8"]),
           reads=["i"], writes=["rsel8"], channel="c", tag="ldrsel")
    emit_bisect(nc, bk, ctx, com, io, B_sb, theta, cnt_ps, msk1)

    with nc.sbuf_tensor("cnt_es", [8, 8], F32) as cnt_es, \
         nc.sbuf_tensor("po_es", [8, 8], F32) as po_es, \
         nc.sbuf_tensor("po_t", [8, 8], F32) as po_t, \
         nc.sbuf_tensor("zr", [128, 2048], F32) as zr:
        bk.op("tensor", lambda t: t.matmul(pb[5][0:8, 0:8], E8sb[:], cnt_ps[:], start=True, stop=True),
              reads=["E8", "cntps"], writes=["pb5"], tag="ces")
        bk.op("scalar", lambda s: s.copy(out=cnt_es[:], in_=pb[5][0:8, 0:8]),
              reads=["pb5"], writes=["cntes"], tag="cese")
        bk.op("vector", lambda v: v.memset(po_es[:], 0.0), writes=["poes"], tag="po0")
        bk.op("vector", lambda v: v.tensor_copy(out=po_es[:, 1:8], in_=cnt_es[:, 0:7]),
              reads=["cntes"], writes=["poes"], tag="posh")
        for k in (1, 2, 4):
            bk.op("vector", lambda v, k=k: v.tensor_copy(out=po_t[:], in_=po_es[:]),
                  reads=["poes"], writes=["pot"], tag=f"poc{k}")
            bk.op("vector", lambda v, k=k: v.tensor_tensor(
                out=po_es[:, k:8], in0=po_t[:, k:8], in1=po_t[:, 0:8 - k], op=ALU.add),
                reads=["pot"], writes=["poes"], tag=f"pop{k}")
        bk.op("vector", lambda v: v.tensor_tensor(out=po_t[:], in0=po_es[:], in1=rsel8[:], op=ALU.mult),
              reads=["poes", "rsel8"], writes=["pot"], tag="posel")
        bk.op("vector", lambda v: v.tensor_reduce(out=po_col[:], in_=po_t[:],
                                                  axis=mybir.AxisListType.X, op=ALU.add),
              reads=["pot"], writes=["pocol"], tag="pored")
        bk.dma("sync", lambda s: s.dma_start(out=io["scrPO"], in_=po_col[:]),
               reads=["pocol"], writes=["scrE:po"], channel="se", tag="stpo")
        bk.dma("gpsimd", lambda g: g.dma_start(out=iota4_sb[:], in_=io["iota4"]),
               reads=["i"], writes=["iota4"], channel="c", tag="ldio4")
        bk.op("vector", lambda v: v.memset(zr[:], 0.0), writes=["zr"], tag="zr")
        for tt in range(20):
            bk.dma("sync", lambda s, tt=tt: s.dma_start(
                out=io["ydram"][tt * 128:(tt + 1) * 128, :], in_=zr[:]),
                reads=["zr"], writes=["ydram"], channel="yz", tag="zy")

    bk.barrier()

    with nc.sbuf_tensor("po1", [1, 1], F32) as po1, \
         nc.sbuf_tensor("po128", [128, 1], F32) as po128, \
         nc.sbuf_tensor("widxf", [128, 4], F32) as widxf, \
         nc.sbuf_tensor("WIu", [128, 4], U32) as WIu, \
         nc.sbuf_tensor("TW", [128, 4, 16], F32) as TW, \
         nc.sbuf_tensor("IDcl", [128, 4], F32) as IDcl, \
         nc.sbuf_tensor("badm", [128, 4], F32) as badm, \
         nc.sbuf_tensor("bad2", [128, 4], F32) as bad2, \
         nc.sbuf_tensor("badu", [128, 4], U32) as badu, \
         nc.sbuf_tensor("trsh", [128, 4], F32) as trsh, \
         nc.sbuf_tensor("IDtu", [128, 4], U32) as IDtu, \
         nc.sbuf_tensor("OW", [128, 2, 2048], BF16) as OW, \
         nc.sbuf_tensor("owf", [128, 2, 2048], F32) as owf:
        bk.op("vector", lambda v: v.tensor_scalar(out=trsh[:], in0=iota4_sb[:], scalar1=2048.0,
                                                  scalar2=None, op0=ALU.add),
              reads=["iota4"], writes=["trsh"], tag="trsh")
        for e in range(8):
            bk.dma("gpsimd", lambda g, e=e: g.dma_start(out=po1[:], in_=io["scrPO"][e:e + 1, 0:1]),
                   reads=["scrE:po"], writes=["po1"], channel="se2", tag="ldpo1")
            com.bcast_row(bk, 2, po1[:], "po1", 1, tag="pob")
            bk.op("scalar", lambda s: s.copy(out=po128[:], in_=pb[2][:, 0:1]),
                  reads=["pb2"], writes=["po128"], tag="poce")
            bk.op("vector", lambda v, e=e: v.tensor_scalar(
                out=widxf[:], in0=iota4_sb[:], scalar1=po128[:], scalar2=float(e * 2048),
                op0=ALU.add, op1=ALU.add),
                reads=["iota4", "po128"], writes=["widxf"], tag="widx")
            bk.op("vector", lambda v: v.tensor_copy(out=WIu[:], in_=widxf[:]),
                  reads=["widxf"], writes=["WIu"], tag="wcast")
            for i in range(4):
                bk.dma("gpsimd", lambda g, i=i: g.indirect_dma_start(
                    out=TW[:, i, :], out_offset=None, in_=io["taual"],
                    in_offset=IOA(WIu[:, i:i + 1], 0)),
                    reads=["WIu", "i:taual"], writes=["TW"], channel="ind", tag="gtw")
            bk.op("vector", lambda v: v.tensor_scalar(
                out=IDcl[:], in0=TW[:, :, 0], scalar1=rbase[:], scalar2=None, op0=ALU.subtract),
                reads=["TW", "rbase"], writes=["IDcl"], tag="dst")
            bk.op("vector", lambda v: v.tensor_scalar(out=badm[:], in0=IDcl[:], scalar1=0.0,
                                                      scalar2=None, op0=ALU.is_lt),
                  reads=["IDcl"], writes=["badm"], tag="blo")
            bk.op("vector", lambda v: v.tensor_scalar(out=bad2[:], in0=IDcl[:], scalar1=2047.5,
                                                      scalar2=None, op0=ALU.is_gt),
                  reads=["IDcl"], writes=["bad2"], tag="bhi")
            bk.op("vector", lambda v: v.tensor_tensor(out=badm[:], in0=badm[:], in1=bad2[:], op=ALU.max),
                  reads=["badm", "bad2"], writes=["badm"], tag="bmax")
            bk.op("vector", lambda v: v.tensor_copy(out=badu[:], in_=badm[:]),
                  reads=["badm"], writes=["badu"], tag="bu")
            bk.op("vector", lambda v: v.copy_predicated(out=IDcl[:], mask=badu[:], data=trsh[:]),
                  reads=["badu", "trsh", "IDcl"], writes=["IDcl"], tag="clamp2")
            bk.op("vector", lambda v: v.tensor_copy(out=IDtu[:], in_=IDcl[:]),
                  reads=["IDcl"], writes=["IDtu"], tag="idcast")
            for i in range(4):
                sl = i % 2
                bk.dma("gpsimd", lambda g, i=i, sl=sl: g.indirect_dma_start(
                    out=OW[:, sl, :], out_offset=None, in_=io["OUTall"],
                    in_offset=IOA(WIu[:, i:i + 1], 0)),
                    reads=["WIu", "i:OUTall"], writes=[f"OW{sl}"], channel="ind", tag="gow")
                bk.op("vector", lambda v, i=i, sl=sl: v.tensor_copy(
                    out=owf[:, sl, :], in_=OW[:, sl, :]),
                    reads=[f"OW{sl}"], writes=[f"owf{sl}"], tag="owc")
                bk.dma("gpsimd", lambda g, i=i, sl=sl: g.indirect_dma_start(
                    out=io["ydram"], out_offset=IOA(IDtu[:, i:i + 1], 0), in_=owf[:, sl, :],
                    in_offset=None, compute_op=ALU.add),
                    reads=["IDtu", f"owf{sl}", "ydram"], writes=["ydram"], channel="ind", tag="sadd")

    bk.barrier()

    with nc.sbuf_tensor("yt", [128, 2048], F32) as yt, \
         nc.sbuf_tensor("x3t", [128, 2048], F32) as x3t, \
         nc.sbuf_tensor("sqt", [128, 2048], F32) as sqt, \
         nc.sbuf_tensor("mcol", [128, 4], F32) as mcol, \
         nc.sbuf_tensor("oout", [128, 2048], F32) as oout:
        for tt in range(16):
            bk.dma("gpsimd", lambda g, tt=tt: g.dma_start(
                out=yt[:], in_=io["ydram"][tt * 128:(tt + 1) * 128, :]),
                reads=["ydram"], writes=["yt"], channel="yl", tag="ldy")
            bk.dma("sync", lambda s, tt=tt: s.dma_start(
                out=x3t[:], in_=io["x3tok"][tt * 128:(tt + 1) * 128, :]),
                reads=["i"], writes=["x3t"], channel="xl", tag="ldx3t")
            bk.op("vector", lambda v: v.tensor_tensor(out=yt[:], in0=yt[:], in1=x3t[:], op=ALU.add),
                  reads=["yt", "x3t"], writes=["yt"], tag="yadd")
            bk.op("vector", lambda v: v.tensor_reduce(out=mcol[:, 0:1], in_=yt[:],
                                                      axis=mybir.AxisListType.X, op=ALU.add),
                  reads=["yt"], writes=["mcol"], tag="ymean")
            bk.op("vector", lambda v: v.tensor_scalar(out=mcol[:, 0:1], in0=mcol[:, 0:1],
                                                      scalar1=1.0 / 2048.0, scalar2=None, op0=ALU.mult),
                  reads=["mcol"], writes=["mcol"], tag="ym2")
            bk.op("vector", lambda v: v.tensor_tensor(out=sqt[:], in0=yt[:], in1=yt[:], op=ALU.mult),
                  reads=["yt"], writes=["sqt"], tag="ysq")
            bk.op("vector", lambda v: v.tensor_reduce(out=mcol[:, 1:2], in_=sqt[:],
                                                      axis=mybir.AxisListType.X, op=ALU.add),
                  reads=["sqt"], writes=["mcol"], tag="ysqs")
            bk.op("vector", lambda v: v.tensor_tensor(out=mcol[:, 2:3], in0=mcol[:, 0:1],
                                                      in1=mcol[:, 0:1], op=ALU.mult),
                  reads=["mcol"], writes=["mcol"], tag="ymsq")
            bk.op("vector", lambda v: v.scalar_tensor_tensor(
                out=mcol[:, 1:2], in0=mcol[:, 1:2], scalar=1.0 / 2048.0, in1=mcol[:, 2:3],
                op0=ALU.mult, op1=ALU.subtract),
                reads=["mcol"], writes=["mcol"], tag="yvar")
            bk.op("vector", lambda v: v.tensor_scalar(out=mcol[:, 1:2], in0=mcol[:, 1:2],
                                                      scalar1=1e-5, scalar2=None, op0=ALU.add),
                  reads=["mcol"], writes=["mcol"], tag="yeps")
            bk.op("scalar", lambda s: s.activation(out=mcol[:, 1:2], in_=mcol[:, 1:2], func=AF.Ln),
                  reads=["mcol"], writes=["mcol"], tag="yln")
            bk.op("scalar", lambda s: s.activation(out=mcol[:, 1:2], in_=mcol[:, 1:2],
                                                   func=AF.Exp, scale=-0.5),
                  reads=["mcol"], writes=["mcol"], tag="yrstd")
            bk.op("vector", lambda v: v.tensor_scalar(
                out=oout[:], in0=yt[:], scalar1=mcol[:, 0:1], scalar2=mcol[:, 1:2],
                op0=ALU.subtract, op1=ALU.mult),
                reads=["yt", "mcol"], writes=["oout"], tag="ynorm")
            bk.dma("sync", lambda s, tt=tt: s.dma_start(
                out=io["y"][tt * 128:(tt + 1) * 128, :], in_=oout[:]),
                reads=["oout"], writes=["o:y"], channel="yo", tag="sty")


from contextlib import ExitStack
from concourse.bass_utils import run_bass_kernel_spmd
from concourse import library_config

NCORE = 8


def _build_a():
    nc = bass.Bass("TRN2", target_bir_lowering=False, debug=False)
    io = {}
    io["xT"] = nc.dram_tensor("xT", [2048, 2048], F32, kind="ExternalInput").ap()
    for nm in ["Wq", "Wk", "Wv", "Wo"]:
        io[nm] = nc.dram_tensor(nm, [2048, 2048], F32, kind="ExternalInput").ap()
    for nm in ["lq1", "lk1", "lq2", "lk2"]:
        io[nm] = nc.dram_tensor(nm, [128, 1], F32, kind="ExternalInput").ap()
    io["subln"] = nc.dram_tensor("subln", [256, 1], F32, kind="ExternalInput").ap()
    io["masks"] = nc.dram_tensor("masks", [4, 128, 512], F32, kind="ExternalInput").ap()
    io["abuf"] = nc.dram_tensor("abuf", [2048, 2048], BF16, kind="Internal").ap()
    io["x1T"] = nc.dram_tensor("x1T", [2048, 2048], F32, kind="ExternalOutput").ap()
    with ExitStack() as ctx:
        bk = BK(nc)
        com = Common(nc, ctx, bk)
        emit_l0_attn(nc, bk, ctx, com, io)
        bk.finish()
    return nc


def _build_b():
    nc = bass.Bass("TRN2", target_bir_lowering=False, debug=False)
    io = {}
    io["x1T"] = nc.dram_tensor("x1T", [2048, 2048], F32, kind="ExternalInput").ap()
    io["SU_ug"] = nc.dram_tensor("SU_ug", [128, 16], F32, kind="ExternalInput").ap()
    io["SU_dp"] = nc.dram_tensor("SU_dp", [128, 64], F32, kind="ExternalInput").ap()
    io["Ws_up"] = nc.dram_tensor("Ws_up", [128, 64], F32, kind="ExternalInput").ap()
    io["Ws_gate"] = nc.dram_tensor("Ws_gate", [128, 64], F32, kind="ExternalInput").ap()
    io["Ws_dp"] = nc.dram_tensor("Ws_dp", [128, 16], F32, kind="ExternalInput").ap()
    io["WupT"] = nc.dram_tensor("WupT", [2048, 8192], F32, kind="ExternalInput").ap()
    io["WgateT"] = nc.dram_tensor("WgateT", [2048, 8192], F32, kind="ExternalInput").ap()
    io["WdownT"] = nc.dram_tensor("WdownT", [8192, 2048], F32, kind="ExternalInput").ap()
    io["H128"] = nc.dram_tensor("H128", [128, 128], F32, kind="ExternalInput").ap()
    io["mT"] = nc.dram_tensor("mT", [8192, 2048], BF16, kind="Internal").ap()
    io["x2T"] = nc.dram_tensor("x2T", [2048, 2048], F32, kind="ExternalOutput").ap()
    with ExitStack() as ctx:
        bk = BK(nc)
        com = Common(nc, ctx, bk)
        emit_l0_ffn(nc, bk, ctx, com, io)
        bk.finish()
    return nc


def _build_c():
    nc = bass.Bass("TRN2", target_bir_lowering=False, debug=False)
    io = {}
    io["x2T"] = nc.dram_tensor("x2T", [2048, 2048], F32, kind="ExternalInput").ap()
    io["sr"] = nc.dram_tensor("sr", [1, 16], F32, kind="ExternalInput").ap()
    io["WqkvT_qk"] = nc.dram_tensor("WqkvT_qk", [2048, 4096], F32, kind="ExternalInput").ap()
    io["WqkvT_v"] = nc.dram_tensor("WqkvT_v", [2048, 2048], F32, kind="ExternalInput").ap()
    io["WgT"] = nc.dram_tensor("WgT", [2048, 2048], F32, kind="ExternalInput").ap()
    io["WoT"] = nc.dram_tensor("WoT", [2048, 2048], F32, kind="ExternalInput").ap()
    io["nw"] = nc.dram_tensor("nw", [128, 16], F32, kind="ExternalInput").ap()
    for nm, shp in [("arr_row", [1, 256]), ("rev_cst", [128, 2]),
                    ("dmn", [128, 2, 256]), ("dmask", [128, 2, 256]), ("ident", [128, 128])]:
        io[nm] = nc.dram_tensor(nm, shp, F32, kind="ExternalInput").ap()
    io["rw"] = nc.dram_tensor("rw", [2048, 8], F32, kind="ExternalInput").ap()
    io["rb"] = nc.dram_tensor("rb", [8, 1], F32, kind="ExternalInput").ap()
    io["gumbelT"] = nc.dram_tensor("gumbelT", [8, 2048], F32, kind="ExternalInput").ap()
    io["obuf"] = nc.dram_tensor("obuf", [2048, 2048], BF16, kind="Internal").ap()
    io["x3T"] = nc.dram_tensor("x3T", [2048, 2048], F32, kind="ExternalOutput").ap()
    io["smperm"] = nc.dram_tensor("smperm", [8, 2048], F32, kind="ExternalOutput").ap()
    with ExitStack() as ctx:
        bk = BK(nc)
        com = Common(nc, ctx, bk)
        emit_l1_attn(nc, bk, ctx, com, io)
        bk.barrier()
        emit_router(nc, bk, ctx, com, io)
        bk.finish()
    return nc


def _build_d():
    nc = bass.Bass("TRN2", target_bir_lowering=False, debug=False)
    nc.gpsimd.load_library(library_config.sparse_gather)
    nc.all_engine_barrier()
    io = {}
    io["smfp"] = nc.dram_tensor("smfp", [64, 2048], F32, kind="ExternalInput").ap()
    io["x3tok"] = nc.dram_tensor("x3tok", [16384, 2048], F32, kind="ExternalInput").ap()
    io["wgate"] = nc.dram_tensor("wgate", [2048, DFFP], F32, kind="ExternalInput").ap()
    io["wup"] = nc.dram_tensor("wup", [2048, DFFP], F32, kind="ExternalInput").ap()
    io["wdown"] = nc.dram_tensor("wdown", [DFFP, 2048], F32, kind="ExternalInput").ap()
    io["bgate"] = nc.dram_tensor("bgate", [128, 43], F32, kind="ExternalInput").ap()
    io["bup"] = nc.dram_tensor("bup", [128, 43], F32, kind="ExternalInput").ap()
    io["bdown"] = nc.dram_tensor("bdown", [1, 2048], F32, kind="ExternalInput").ap()
    for nm, shp in [("E8", [128, 8]), ("tauc", [128, 1024]), ("identf", [128, 128]),
                    ("iotah", [16, 128]), ("iota16", [128, 16])]:
        io[nm] = nc.dram_tensor(nm, shp, F32, kind="ExternalInput").ap()
    io["rowsel"] = nc.dram_tensor("rowsel", [16, 1], U32, kind="ExternalInput").ap()
    io["scr"] = nc.dram_tensor("scr", [128, 1024], F32, kind="Internal").ap()
    io["scr2"] = nc.dram_tensor("scr2", [128, 1024], F32, kind="Internal").ap()
    io["scrF2"] = nc.dram_tensor("scrF2", [128, 16], F32, kind="Internal").ap()
    io["me"] = nc.dram_tensor("me", [DFFP, 2048], BF16, kind="Internal").ap()
    io["OUT"] = nc.dram_tensor("OUT", [2048, 2048], BF16, kind="ExternalOutput").ap()
    io["taub"] = nc.dram_tensor("taub", [2048, 16], F32, kind="ExternalOutput").ap()
    with ExitStack() as ctx:
        bk = BK(nc)
        com = Common(nc, ctx, bk)
        emit_expert(nc, bk, ctx, com, io)
        bk.finish()
    mybir.codegen_inst_isa_subclasses(nc)
    return nc


def _build_e():
    nc = bass.Bass("TRN2", target_bir_lowering=False, debug=False)
    io = {}
    io["smfp"] = nc.dram_tensor("smfp", [64, 2048], F32, kind="ExternalInput").ap()
    io["OUTall"] = nc.dram_tensor("OUTall", [16896, 2048], BF16, kind="ExternalInput").ap()
    io["taual"] = nc.dram_tensor("taual", [16896, 16], F32, kind="ExternalInput").ap()
    io["x3tok"] = nc.dram_tensor("x3tok", [2048, 2048], F32, kind="ExternalInput").ap()
    io["rbase"] = nc.dram_tensor("rbase", [128, 1], F32, kind="ExternalInput").ap()
    io["rsel8"] = nc.dram_tensor("rsel8", [8, 8], F32, kind="ExternalInput").ap()
    for nm, shp in [("E8", [128, 8]), ("iota4", [128, 4])]:
        io[nm] = nc.dram_tensor(nm, shp, F32, kind="ExternalInput").ap()
    io["scrE"] = nc.dram_tensor("scrE", [128, 1024], F32, kind="Internal").ap()
    io["scrPO"] = nc.dram_tensor("scrPO", [8, 1], F32, kind="Internal").ap()
    io["ydram"] = nc.dram_tensor("ydram", [2560, 2048], F32, kind="Internal").ap()
    io["y"] = nc.dram_tensor("y", [2048, 2048], F32, kind="ExternalOutput").ap()
    with ExitStack() as ctx:
        bk = BK(nc)
        com = Common(nc, ctx, bk)
        emit_scatter_ln(nc, bk, ctx, com, io)
        bk.finish()
    return nc


def _gumbel_const():
    old = np.seterr(over="ignore")
    n = 16384 * 8
    x0 = np.zeros(n, dtype=np.uint32)
    x1 = np.arange(n, dtype=np.uint32)
    rotations = [[13, 15, 26, 6], [17, 29, 16, 24]]
    ks = [np.uint32(0), np.uint32(42), np.uint32(0) ^ np.uint32(42) ^ np.uint32(0x1BD11BDA)]
    rotl = lambda v, r: ((v << np.uint32(r)) | (v >> np.uint32(32 - r))).astype(np.uint32)
    x0 = (x0 + ks[0]).astype(np.uint32)
    x1 = (x1 + ks[1]).astype(np.uint32)
    for i in range(5):
        for r in rotations[i % 2]:
            x0 = (x0 + x1).astype(np.uint32)
            x1 = rotl(x1, r)
            x1 = x1 ^ x0
        x0 = (x0 + ks[(i + 1) % 3]).astype(np.uint32)
        x1 = (x1 + ks[(i + 2) % 3] + np.uint32(i + 1)).astype(np.uint32)
    bits = (x0 ^ x1).astype(np.uint32)
    u = (((bits >> np.uint32(9)) | np.uint32(0x3F800000)).view(np.float32) - 1.0).reshape(16384, 8)
    np.seterr(**old)
    g = -np.log(-np.log(u.astype(np.float64) + 1e-10) + 1e-10)
    return np.asarray(g, dtype=np.float32)


def kernel(**inputs):
    cc = np.ascontiguousarray
    x = np.asarray(inputs["L0_attn_x"], dtype=np.float32)
    masks = host_constants_a()
    H128 = host_constants_b()
    ccst = host_constants_c()
    cd = host_constants_d()
    gumbel = _gumbel_const()

    cv = lambda v: cc(np.asarray(v, np.float32).reshape(-1, 128).T)
    Wq, Wk, Wv, Wo0 = (np.asarray(inputs[f"L0_attn_W{n}"], np.float32) for n in "qkvo")
    su_ug = cv(inputs["L0_ffn_SU_ug"])
    su_dp = cv(inputs["L0_ffn_SU_dp"])
    ws_ug = np.asarray(inputs["L0_ffn_Wscale_ug"], np.float32)
    ws_up, ws_gate = cv(ws_ug[:8192]), cv(ws_ug[8192:])
    ws_dp = cv(inputs["L0_ffn_Wscale_dp"])
    WupT = cc(np.asarray(inputs["L0_ffn_W_up"], np.float32).T)
    WgateT = cc(np.asarray(inputs["L0_ffn_W_gate"], np.float32).T)
    WdownT = cc(np.asarray(inputs["L0_ffn_W_down"], np.float32).T)
    Wqkv = np.asarray(inputs["L1_attn_Wqkv"], np.float32).reshape(16, 3, 128, 2048)
    qk_cols = cc(np.concatenate([np.concatenate([Wqkv[h, 0], Wqkv[h, 1]], axis=0)
                                 for h in range(16)], axis=0).T)
    v_cols = cc(np.concatenate([Wqkv[h, 2] for h in range(16)], axis=0).T)
    WgT1 = cc(np.asarray(inputs["L1_attn_Wg"], np.float32).T)
    WoT1 = cc(np.asarray(inputs["L1_attn_Wo"], np.float32).T)
    nw1 = cc(np.asarray(inputs["L1_attn_norm_w"], np.float32).reshape(16, 128).T)
    sr1 = np.asarray(inputs["L1_attn_slope_rate"], np.float32).reshape(1, 16)
    rw = np.asarray(inputs["L1_ffn_router_w"], np.float32)
    rb = np.asarray(inputs["L1_ffn_router_b"], np.float32).reshape(8, 1)
    pad = DFFP - 5461
    wgate_p = cc(np.pad(np.asarray(inputs["L1_ffn_w_gate"], np.float32), ((0, 0), (0, pad))))
    wup_p = cc(np.pad(np.asarray(inputs["L1_ffn_w_up"], np.float32), ((0, 0), (0, pad))))
    wdown_p = cc(np.pad(np.asarray(inputs["L1_ffn_w_down"], np.float32), ((0, pad), (0, 0))))
    bg_p = cc(np.pad(np.asarray(inputs["L1_ffn_b_gate"], np.float32), (0, pad)).reshape(43, 128).T)
    bu_p = cc(np.pad(np.asarray(inputs["L1_ffn_b_up"], np.float32), (0, pad)).reshape(43, 128).T)
    bd1 = np.asarray(inputs["L1_ffn_b_down"], np.float32).reshape(1, 2048)
    lq1 = np.asarray(inputs["L0_attn_lambda_q1"], np.float32).reshape(128, 1)
    lk1 = np.asarray(inputs["L0_attn_lambda_k1"], np.float32).reshape(128, 1)
    lq2 = np.asarray(inputs["L0_attn_lambda_q2"], np.float32).reshape(128, 1)
    lk2 = np.asarray(inputs["L0_attn_lambda_k2"], np.float32).reshape(128, 1)
    subln = np.asarray(inputs["L0_attn_subln_weight"], np.float32).reshape(256, 1)

    ncA = _build_a()
    inA = [dict(xT=cc(x[c].T), Wq=Wq, Wk=Wk, Wv=Wv, Wo=Wo0, lq1=lq1, lk1=lk1,
                lq2=lq2, lk2=lk2, subln=subln, masks=masks) for c in range(NCORE)]
    rA = run_bass_kernel_spmd(ncA, inA, core_ids=list(range(NCORE)))
    x1T = [np.asarray(rA.results[c]["x1T"]) for c in range(NCORE)]

    ncB = _build_b()
    inB = [dict(x1T=x1T[c], SU_ug=su_ug, SU_dp=su_dp, Ws_up=ws_up, Ws_gate=ws_gate,
                Ws_dp=ws_dp, WupT=WupT, WgateT=WgateT, WdownT=WdownT, H128=H128)
           for c in range(NCORE)]
    rB = run_bass_kernel_spmd(ncB, inB, core_ids=list(range(NCORE)))
    x2T = [np.asarray(rB.results[c]["x2T"]) for c in range(NCORE)]

    ncC = _build_c()
    inC = [dict(x2T=x2T[c], sr=sr1, WqkvT_qk=qk_cols, WqkvT_v=v_cols, WgT=WgT1,
                WoT=WoT1, nw=nw1, rw=rw, rb=rb,
                gumbelT=cc(gumbel[c * 2048:(c + 1) * 2048, :].T), **ccst)
           for c in range(NCORE)]
    rC = run_bass_kernel_spmd(ncC, inC, core_ids=list(range(NCORE)))
    x3T = [np.asarray(rC.results[c]["x3T"]) for c in range(NCORE)]
    smfp = np.concatenate([np.asarray(rC.results[c]["smperm"]) for c in range(NCORE)], axis=0)

    x3tok = cc(np.concatenate([x3T[c].T for c in range(NCORE)], axis=0))
    ncD = _build_d()
    inD = [dict(smfp=smfp, x3tok=x3tok, wgate=wgate_p, wup=wup_p, wdown=wdown_p,
                bgate=bg_p, bup=bu_p, bdown=bd1, E8=cd["E8"], tauc=cd["tauc"],
                identf=cd["identf"], iotah=cd["iotah"], iota16=cd["iota16"],
                rowsel=(np.arange(16, dtype=np.uint32) + 16 * c).reshape(16, 1))
           for c in range(NCORE)]
    rD = run_bass_kernel_spmd(ncD, inD, core_ids=list(range(NCORE)))
    OUTs = [np.asarray(rD.results[c]["OUT"]) for c in range(NCORE)]
    TAUs = [np.asarray(rD.results[c]["taub"]) for c in range(NCORE)]

    OUTall = np.concatenate(OUTs + [np.zeros((512, 2048), dtype=OUTs[0].dtype)], axis=0)
    taual = np.concatenate(TAUs + [np.zeros((512, 16), dtype=np.float32)], axis=0)
    ncE = _build_e()
    inE = []
    for c in range(NCORE):
        rs8 = np.zeros((8, 8), dtype=np.float32)
        rs8[:, c] = 1.0
        inE.append(dict(smfp=smfp, OUTall=OUTall, taual=taual, x3tok=cc(x3T[c].T),
                        rbase=np.full((128, 1), c * 2048.0, dtype=np.float32),
                        rsel8=rs8, E8=cd["E8"], iota4=cd["iota4"]))
    rE = run_bass_kernel_spmd(ncE, inE, core_ids=list(range(NCORE)))
    y = np.stack([np.asarray(rE.results[c]["y"]) for c in range(NCORE)], axis=0)
    return np.ascontiguousarray(y.astype(np.float32))
